# version 13
# baseline (speedup 1.0000x reference)
from contextlib import ExitStack

import numpy as np
import ml_dtypes

import concourse.bass as bass
import concourse.mybir as mybir
from concourse.bass_utils import run_bass_kernel_spmd

F32 = mybir.dt.float32
BF16 = mybir.dt.bfloat16
AX = mybir.AxisListType.X
OP = mybir.AluOpType
AF = mybir.ActivationFunctionType

D = 1024
NC8 = 8
HEADS = 16
C_ROUND = 12582912.0
EPS = float(np.float32(1e-5))
F = np.float32


class Ctr:
    def __init__(self, stack, nc, name):
        self.h = stack.enter_context(nc.semaphore(name))
        self.n = 0

    def inc(self, k=1):
        self.n += k
        return self.n


def build_nc(T=4096, n_cores=8, stages=3):
    NT = T // 128
    NG = NT // 4
    nc = bass.Bass("TRN2", target_bir_lowering=False, debug=False)

    x_in = nc.dram_tensor("x_in", [T, D], F32, kind="ExternalInput").ap()
    wqkvT_h = nc.dram_tensor("wqkvT", [D, 3 * D], F32, kind="ExternalInput").ap()
    wprojT_h = nc.dram_tensor("wprojT", [D, D], F32, kind="ExternalInput").ap()
    wffnT_h = nc.dram_tensor("wffnT", [D, D], F32, kind="ExternalInput").ap()
    gcols_h = nc.dram_tensor("gcolsi", [128, NC8], F32, kind="ExternalInput").ap()
    bcols_h = nc.dram_tensor("bcolsi", [128, NC8], F32, kind="ExternalInput").ap()
    bprojr_h = nc.dram_tensor("bprojr", [1, D], F32, kind="ExternalInput").ap()
    identF_h = nc.dram_tensor("identF", [128, 128], F32, kind="ExternalInput").ap()
    identB_h = nc.dram_tensor("identB", [128, 128], BF16, kind="ExternalInput").ap()
    ones1_h = nc.dram_tensor("ones1", [1, 128], BF16, kind="ExternalInput").ap()
    out_h = nc.dram_tensor("out", [T, D], F32, kind="ExternalOutput").ap()

    t_d = nc.dram_tensor("t_scr", [T, D], F32).ap()
    x1_d = nc.dram_tensor("x1_scr", [T, D], F32).ap()
    x2_d = nc.dram_tensor("x2_scr", [T, D], F32).ap()
    cc_in = [nc.dram_tensor(f"cc_in{i}", [1, 128], F32).ap() for i in range(2)]
    cc_out = [nc.dram_tensor(f"cc_out{i}", [1, 128], F32).ap() for i in range(2)]

    xt = x_in.rearrange("(n p) d -> n p d", p=128)
    tt = t_d.rearrange("(n p) d -> n p d", p=128)
    x1t = x1_d.rearrange("(n p) d -> n p d", p=128)
    x2t = x2_d.rearrange("(n p) d -> n p d", p=128)
    outt = out_h.rearrange("(n p) d -> n p d", p=128)

    stack = ExitStack()
    e = stack.enter_context

    NXW = 8
    xw = [e(nc.sbuf_tensor(f"xw{i}", [128, D], F32)) for i in range(NXW)]
    tTg = e(nc.sbuf_tensor("tTg", [128, 4 * D], F32))
    nTg = e(nc.sbuf_tensor("nTg", [128, 4 * D], BF16))
    wsgn = e(nc.sbuf_tensor("wsgn", [128, NC8, D], BF16))
    wqkv = e(nc.sbuf_tensor("wqkv", [128, NC8, 3 * D], BF16))
    wproj = e(nc.sbuf_tensor("wproj", [128, NC8, D], BF16))
    scrA = e(nc.sbuf_tensor("scrA", [128, D], F32))
    scrB = e(nc.sbuf_tensor("scrB", [128, D], F32))
    h2t = [e(nc.sbuf_tensor(f"h2t{i}", [128, D], BF16)) for i in range(2)]
    h2Tt = [e(nc.sbuf_tensor(f"h2Tt{i}", [128, D], BF16)) for i in range(4)]
    h2Tg = e(nc.sbuf_tensor("h2Tg", [128, 4 * D], BF16))

    ktile = [e(nc.sbuf_tensor(f"ktile{i}", [128, D], BF16)) for i in range(2)]
    vsb = [e(nc.sbuf_tensor(f"vsb{i}", [128, HEADS * 65], BF16)) for i in range(2)]
    kvf = e(nc.sbuf_tensor("kvf", [128, NC8 * 130], F32))
    kvb = e(nc.sbuf_tensor("kvb", [128, NC8 * 130], BF16))
    st_muv = e(nc.sbuf_tensor("st_muv", [128, 2 * NT], F32))
    st_A = e(nc.sbuf_tensor("st_A", [128, NT], F32))
    st_B = e(nc.sbuf_tensor("st_B", [128, NT], F32))
    st_invA = e(nc.sbuf_tensor("st_invA", [128, NT], F32))
    st_cA = e(nc.sbuf_tensor("st_cA", [128, NT], F32))
    s2_muv = e(nc.sbuf_tensor("s2_muv", [128, 2 * NT], F32))
    s2_A = e(nc.sbuf_tensor("s2_A", [128, NT], F32))
    s2_B = e(nc.sbuf_tensor("s2_B", [128, NT], F32))
    stw = [e(nc.sbuf_tensor(f"stw{i}", [128, 4], F32)) for i in range(5)]
    bn12 = e(nc.sbuf_tensor("bn12", [128, 12], F32))
    maxcol = e(nc.sbuf_tensor("maxcol", [128, max(NG, 1)], F32))
    smax = e(nc.sbuf_tensor("smax", [128, 1], F32))
    grow = e(nc.sbuf_tensor("grow", [1, 128], F32))
    arow = e(nc.sbuf_tensor("arow", [1, 256], F32))
    drow = e(nc.sbuf_tensor("drow", [1, 128], F32))
    brow = e(nc.sbuf_tensor("brow", [1, 256], F32))
    s11 = e(nc.sbuf_tensor("s11", [1, 8], F32))
    invscol = e(nc.sbuf_tensor("invscol", [128, 1], F32))
    hafcol = e(nc.sbuf_tensor("hafcol", [128, 1], F32))
    gscols = e(nc.sbuf_tensor("gscols", [128, NC8], F32))
    bsccols = e(nc.sbuf_tensor("bsccols", [128, NC8], F32))
    gcols = e(nc.sbuf_tensor("gcols", [128, NC8], F32))
    bcols = e(nc.sbuf_tensor("bcols", [128, NC8], F32))
    sumabs = e(nc.sbuf_tensor("sumabs", [128, NC8], F32))
    cntc = e(nc.sbuf_tensor("cntc", [128, NC8], F32))
    msumc = e(nc.sbuf_tensor("msumc", [128, NC8], F32))
    al2 = e(nc.sbuf_tensor("al2", [128, 2], F32))
    deltac = e(nc.sbuf_tensor("deltac", [128, 1], F32))
    identF = e(nc.sbuf_tensor("identFs", [128, 128], F32))
    identB = e(nc.sbuf_tensor("identBs", [128, 128], BF16))
    ones1 = e(nc.sbuf_tensor("ones1s", [1, 128], BF16))
    bprojb = e(nc.sbuf_tensor("bprojb", [1, D], BF16))
    rcb = e(nc.sbuf_tensor("rcb", [128, 8], F32))
    rrb = e(nc.sbuf_tensor("rrb", [128, 8], F32))
    ccst = e(nc.sbuf_tensor("ccst", [128, 4], F32))

    qg = nTg
    attng_ap = tTg.ap().bitcast(BF16)
    wstage_ap = wqkv.ap().rearrange("p a b -> p (a b)").bitcast(F32)
    scrC_ap = nTg.ap().bitcast(F32)

    PS = e(nc.psum_tensor("PS", [128, 4096], F32))

    def bank(b, lo=0, hi=512):
        return PS[:, 512 * b + lo : 512 * b + hi]

    def bankpair(b):
        return PS.ap()[:, 512 * b : 512 * b + 1024]

    def bankb16(b):
        return PS.ap()[:, 512 * b : 512 * (b + 1)].bitcast(BF16)

    def rep0(src11, n):
        a = src11
        return bass.AP(tensor=a.tensor, offset=a.offset,
                       ap=[list(a.ap[0]), [0, n]])

    def strided1(src, step, count):
        a = src
        return bass.AP(tensor=a.tensor, offset=a.offset,
                       ap=[list(a.ap[0]), [step, count]])

    sems = {}
    by_handle = {}
    blk = e(nc.Block())

    def S(name):
        if name not in sems:
            c = Ctr(stack, nc, name)
            sems[name] = c
            by_handle[id(c.h)] = c
        return sems[name]

    _CHAIN_OPS = {
        "tensor_scalar", "tensor_tensor", "scalar_tensor_tensor", "tensor_copy",
        "reciprocal", "reduce_max", "reduce_sum", "bn_stats", "bn_aggr",
        "memset", "tensor_scalar_add", "tensor_scalar_mul", "activation",
        "copy",
    }
    chain_state = {}

    class _Shim:
        __slots__ = ("host", "ins")

        def __init__(self, host, ins):
            self.host = host
            self.ins = ins

        def then_inc(self, semh, k):
            self.host._attach(self.ins, semh, k)
            return self.ins

    class ChainEng:
        def __init__(self, eng, key):
            self._e = eng
            self._key = key
            if key not in chain_state:
                chain_state[key] = {"last": None, "pending": None,
                                    "ctr": S(f"chain_{key}")}

        def _flush(self):
            st = chain_state[self._key]
            if st["pending"] is not None:
                ctr = st["ctr"]
                st["pending"].then_inc(ctr.h, 1)
                ctr.inc()
                st["last"] = (ctr.h, ctr.n)
                st["pending"] = None

        def _attach(self, ins, semh, k):
            st = chain_state[self._key]
            ctr = by_handle[id(semh)]
            ins.then_inc(semh, k)
            st["last"] = (semh, ctr.n + k)
            if st["pending"] is ins:
                st["pending"] = None

        def wait_ge(self, h, v):
            self._e.wait_ge(h, v)

        def drain(self):
            st = chain_state[self._key]
            if st["pending"] is not None:
                return _Shim(self, st["pending"])
            ins = self._e.drain()
            st["pending"] = ins
            return _Shim(self, ins)

        def __getattr__(self, name):
            raw = getattr(self._e, name)
            if name not in _CHAIN_OPS:
                return raw

            def emit(*a, **kw):
                self._flush()
                st = chain_state[self._key]
                if st["last"] is not None:
                    self._e.wait_ge(st["last"][0], st["last"][1])
                ins = raw(*a, **kw)
                st["pending"] = ins
                return _Shim(self, ins)

            return emit

    import os
    _seclim = int(os.environ.get("KSEC_LIMIT", "9999"))
    KF1 = os.environ.get("KF1", "1") == "1"
    KF2 = os.environ.get("KF2", "1") == "1"
    _secn = [0]

    def sec(dec, fn):
        _secn[0] += 1
        if _secn[0] <= _seclim:
            if getattr(dec, "__name__", "") == "scalar":
                dec(lambda e2: fn(ChainEng(e2, "s")))
            else:
                dec(fn)

    s_ld = [S(f"s_ld{k}") for k in range(NXW)]
    s_st4 = [S(f"s_st{k}") for k in range(NXW)]
    s_ws = [S(f"s_ws{k}") for k in range(2)]

    wst = wstage_ap[:, 0:2048].rearrange("p (s k) -> p s k", s=2)
    tTg_cw = tTg.ap().rearrange("p (c w) -> p c w", c=NC8)
    h2Tg_cw = h2Tg.ap().rearrange("p (c w) -> p c w", c=NC8)

    s_const = S("s_const")

    def ph0_sync(sy):
        for dst, src in [
            (gcols[:], gcols_h[:, :]), (bcols[:], bcols_h[:, :]),
            (identF[:], identF_h[:, :]), (identB[:], identB_h[:, :]),
            (ones1[:], ones1_h[:, :]),
        ]:
            sy.dma_start(out=dst, in_=src).then_inc(s_const.h, 16)
            s_const.inc(16)

    sec(blk.sync, ph0_sync)
    CONSTS = s_const.n

    s_zr = S("s_zr")

    def ph0_vec(v):
        v.memset(kvf[:], 0.0)
        v.memset(ccst[:, 0:1], EPS)
        v.memset(ccst[:, 1:2], float(np.float32(EPS) * np.float32(EPS)))
        v.memset(ccst[:, 2:3], float(np.float32(1.0) + np.float32(EPS)))
        v.memset(ccst[:, 3:4], -1.0)
        for i in range(2):
            v.memset(vsb[i].ap().rearrange("p (h c) -> p h c", c=65)[:, :, 64:65], 1.0)
        v.drain().then_inc(s_zr.h, 1)
        s_zr.inc()

    sec(blk.vector, ph0_vec)

    s_bpj = S("s_bpj")

    def ph0_gp(g):
        g.dma_start(out=bprojb[:], in_=bprojr_h[:, :]).then_inc(s_bpj.h, 16)
        s_bpj.inc(16)

    sec(blk.gpsimd, ph0_gp)

    s_wf1 = S("s_wf1")
    s_wf1u = S("s_wf1u")
    s_wf2 = S("s_wf2")
    s_wf2u = S("s_wf2u")
    s_abs1 = S("s_abs1")
    s_dl1 = S("s_dl1")
    s_dl2 = S("s_dl2")
    s_dl3 = S("s_dl3")
    s_dl4 = S("s_dl4")
    s_as2 = S("s_as2")
    s_msk = S("s_msk")
    s_wsg = S("s_wsg")
    s_al1 = S("s_al1")
    s_alf = S("s_alf")

    def wffn_gp(g):
        for c in range(NC8):
            if c >= 2:
                g.wait_ge(s_abs1.h, c - 1)
            g.dma_start(
                out=wst[:, c % 2, :], in_=wffnT_h[128 * c : 128 * (c + 1), :]
            ).then_inc(s_ws[c % 2].h, 16)
            s_ws[c % 2].inc(16)
        g.wait_ge(s_dl1.h, 1)
        g.dma_start(out=drow[0:1, :], in_=deltac[:, 0:1]).then_inc(s_dl2.h, 16)
        s_dl2.inc(16)
        g.wait_ge(s_dl3.h, 1)
        g.dma_start(out=deltac[:, 0:1], in_=rep0(drow.ap()[0:1, 0:1], 128)).then_inc(
            s_dl4.h, 16
        )
        s_dl4.inc(16)
        for c in range(NC8):
            if c == 0:
                g.wait_ge(s_abs1.h, NC8)
            if c >= 2:
                g.wait_ge(s_as2.h, c - 1)
            g.dma_start(
                out=wst[:, c % 2, :], in_=wffnT_h[128 * c : 128 * (c + 1), :]
            ).then_inc(s_ws[c % 2].h, 16)
            s_ws[c % 2].inc(16)
        g.wait_ge(s_al1.h, 1)
        g.dma_start(out=arow[0:1, :], in_=al2[:, :]).then_inc(s_alf.h, 16)
        s_alf.inc(16)

    sec(blk.gpsimd, wffn_gp)

    def wffn_sc(s):
        ws_base = [16 * 4 * 0, 0]
        for c in range(NC8):
            s.wait_ge(s_ws[c % 2].h, 16 * (c // 2 + 1))
            s.activation(
                out=scrA[:], in_=wst[:, c % 2, :], func=AF.Abs,
                accum_out=sumabs[:, c : c + 1],
            ).then_inc(s_abs1.h, 1)
            s_abs1.inc()
            s.drain()
        for c in range(NC8):
            s.wait_ge(s_ws[c % 2].h, 16 * 4 + 16 * (c // 2 + 1))
            if c >= 1:
                s.wait_ge(s_msk.h, c)
            s.activation(out=scrA[:], in_=wst[:, c % 2, :], func=AF.Abs)
            s.activation(out=scrB[:], in_=wst[:, c % 2, :], func=AF.Sign).then_inc(
                s_as2.h, 1
            )
            s_as2.inc()
            s.nop().then_inc(s_wf2u.h, 1)
            s_wf2u.inc()

    sec(blk.scalar, wffn_sc)

    def wffn_vec(v):
        v.wait_ge(s_abs1.h, NC8)
        v.reduce_sum(out=deltac[:], in_=sumabs[:], axis=AX)
        v.drain().then_inc(s_dl1.h, 1)
        s_dl1.inc()
        v.wait_ge(s_dl2.h, 16)
        v.reduce_sum(out=s11[0:1, 7:8], in_=drow[0:1, :], axis=AX)
        v.drain()
        v.tensor_scalar(
            out=drow[0:1, 0:1], in0=s11[0:1, 7:8],
            scalar1=float(np.float32(0.7)) / (D * D), scalar2=None, op0=OP.mult,
        )
        v.drain().then_inc(s_dl3.h, 1)
        s_dl3.inc()
        for c in range(NC8):
            v.wait_ge(s_as2.h, c + 1)
            if c == 0:
                v.wait_ge(s_dl4.h, 16)
            v.scalar_tensor_tensor(
                out=scrC_ap[:, 0:D], in0=scrA[:], scalar=deltac[:, 0:1], in1=scrA[:],
                op0=OP.is_gt, op1=OP.mult, accum_out=msumc[:, c : c + 1],
            )
            v.drain()
            v.tensor_scalar(
                out=scrC_ap[:, 0:D], in0=scrA[:], scalar1=deltac[:, 0:1], scalar2=0.0,
                op0=OP.is_gt, op1=OP.add, accum_out=cntc[:, c : c + 1],
            )
            v.drain()
            v.tensor_tensor(out=wsgn[:, c, :], in0=scrC_ap[:, 0:D], in1=scrB[:], op=OP.mult)
            v.drain().then_inc(s_wsg.h, 1)
            s_wsg.inc()
            v.drain().then_inc(s_msk.h, 1)
            s_msk.inc()
        v.reduce_sum(out=al2[:, 0:1], in_=msumc[:], axis=AX)
        v.drain()
        v.reduce_sum(out=al2[:, 1:2], in_=cntc[:], axis=AX)
        v.drain().then_inc(s_al1.h, 1)
        s_al1.inc()

    sec(blk.vector, wffn_vec)

    def emit_bn(tag, src_tiles, muv, sA, sB, sInvA, two_r, gate=None,
                slot_gate=None, vec_tile=None, vec_tail=None, pre_stats=None):
        s_bn = S(f"sbn_{tag}")
        bases = dict(bn=s_bn.n, ld=[s_ld[k].n for k in range(NXW)])

        def f_sync(sy):
            for i in range(NT):
                if gate is not None:
                    gate(sy, i)
                if i >= NXW and slot_gate is not None:
                    slot_gate(sy, i)
                sy.dma_start(out=xw[i % NXW][:], in_=src_tiles[i]).then_inc(
                    s_ld[i % NXW].h, 16
                )
                s_ld[i % NXW].inc(16)

        s_iv = S(f"siv_{tag}")
        s_stA = S(f"sstA_{tag}")
        bases["iv"] = s_iv.n
        bases["stA"] = s_stA.n

        def recip_tile(v, j):
            v.wait_ge(s_iv.h, bases["iv"] + j + 1)
            v.reciprocal(out=sA[:, j : j + 1], in_=sInvA[:, j : j + 1])
            v.drain().then_inc(s_stA.h, 1)
            s_stA.inc()

        def f_vec(v):
            for i in range(NT):
                if pre_stats is None:
                    v.wait_ge(s_ld[i % NXW].h,
                              bases["ld"][i % NXW] + 16 * (i // NXW + 1))
                    v.bn_stats(out=bn12[:, 0:6], in_=xw[i % NXW][:, 0:512])
                    v.bn_stats(out=bn12[:, 6:12], in_=xw[i % NXW][:, 512:1024])
                    v.drain()
                    v.bn_aggr(out=muv[:, 2 * i : 2 * i + 2], in_=bn12[:])
                    v.drain().then_inc(s_bn.h, 1)
                    s_bn.inc()
                    if i >= 1:
                        recip_tile(v, i - 1)
                if vec_tile is not None:
                    vec_tile(v, i)
            if pre_stats is None:
                recip_tile(v, NT - 1)
            if vec_tail is not None:
                vec_tail(v)

        sec(blk.sync, f_sync)
        sec(blk.vector, f_vec)

        if pre_stats is not None:
            pre_sem, pre_base = pre_stats

            def sc_wait_pre(s, i):
                s.wait_ge(pre_sem.h, pre_base + i + 1)
                s.wait_ge(s_ld[i % NXW].h,
                          bases["ld"][i % NXW] + 16 * (i // NXW + 1))

            return bases, None, sc_wait_pre

        def sc_inv(s, i):
            s.wait_ge(s_bn.h, bases["bn"] + i + 1)
            vv = muv.ap()[:, 2 * i + 1 : 2 * i + 2]
            if two_r:
                s.activation(out=sInvA[:, i : i + 1], in_=vv, func=AF.Sqrt,
                             scale=ccst[:, 2:3], bias=ccst[:, 1:2]
                             ).then_inc(s_iv.h, 1)
            else:
                s.activation(out=sInvA[:, i : i + 1], in_=vv, func=AF.Sqrt,
                             bias=ccst[:, 0:1]).then_inc(s_iv.h, 1)
            s_iv.inc()

        def sc_finish(s, j):
            s.wait_ge(s_stA.h, bases["stA"] + j + 1)
            w2 = stw[2].ap()[:, j % 4 : j % 4 + 1]
            mu = muv.ap()[:, 2 * j : 2 * j + 1]
            s.activation(out=w2, in_=mu, func=AF.Identity, scale=ccst[:, 3:4])
            s.activation(out=sB[:, j : j + 1], in_=w2, func=AF.Identity,
                         scale=sA.ap()[:, j : j + 1])

        return bases, sc_inv, sc_finish

    def emit_ffn_stage(stage_no, src_tiles, dst_tiles, cc_idx, gate, compute_alpha,
                       psum_pred=None, post_vec=None, post_slot_gate=None,
                       post_sc_group=None, pre_stats=None):
        tag = "fS"
        bstA = [s_st4[k].n for k in range(NXW)]
        s_t = S(f"st_{tag}")
        s_tp = S(f"stp_{tag}")
        s_tev = S(f"stev_{tag}")
        s_xn = S(f"sxn_{tag}")
        s_red = S(f"sred_{tag}")
        s_mx = S(f"smx_{tag}")
        b = dict(t=s_t.n, tp=s_tp.n, tev=s_tev.n, xn=s_xn.n,
                 red=s_red.n, mx=s_mx.n)

        def slot_gate(sy, i):
            if KF1:
                sy.wait_ge(s_tp.h, b["tp"] + i - NXW + 1)
            sy.wait_ge(s_st4[i % NXW].h, bstA[i % NXW] + 16 * (i // NXW))

        def red_group(v, g):
            v.wait_ge(s_xn.h, b["xn"] + g + 1)
            v.reduce_max(
                out=maxcol[:, g : g + 1], in_=tTg[:],
                axis=AX, apply_absolute_value=True,
            )
            v.drain().then_inc(s_red.h, 1)
            s_red.inc()

        ngdone = [0]

        def vec_tile_f(v, i):
            if not KF1:
                return
            if i >= 6 and (i - 6) % 4 == 0:
                red_group(v, (i - 6) // 4)
                ngdone[0] += 1

        def vec_tail(v):
            if KF1:
                for g in range(ngdone[0], NG):
                    red_group(v, g)
                v.reduce_max(out=smax[:], in_=maxcol[:], axis=AX,
                             apply_absolute_value=True)
                v.drain().then_inc(s_mx.h, 1)
                s_mx.inc()

        bnb, sc_inv, sc_finish = emit_bn(
            tag + "a", src_tiles, st_muv, st_A, st_B, st_invA, True, gate,
            slot_gate=slot_gate, vec_tile=vec_tile_f, vec_tail=vec_tail,
            pre_stats=pre_stats,
        )

        def fA_sc(s):
            s.wait_ge(s_const.h, CONSTS)
            s.wait_ge(s_zr.h, 1)

            def xn_block(g):
                for c in range(NC8):
                    act = s.activation(
                        out=tTg[:, 512 * c : 512 * (c + 1)],
                        in_=tTg[:, 512 * c : 512 * (c + 1)],
                        func=AF.Identity,
                        scale=gcols[:, c : c + 1], bias=bcols[:, c : c + 1],
                    )
                    if c == NC8 - 1:
                        act.then_inc(s_xn.h, 1)
                s_xn.inc()

            def evac(j):
                if j % 4 == 0 and j >= 4:
                    s.wait_ge(s_red.h, b["red"] + j // 4)
                s.wait_ge(s_tp.h, b["tp"] + j + 1)
                s.activation(
                    out=tTg_cw[:, :, 128 * (j % 4) : 128 * (j % 4 + 1)],
                    in_=bankpair((j % 4) * 2), func=AF.Identity,
                ).then_inc(s_tev.h, 1)
                s_tev.inc()
                if j % 4 == 3:
                    xn_block(j // 4)

            nev = [0]

            def advance(upto):
                while KF1 and nev[0] <= upto:
                    evac(nev[0])
                    nev[0] += 1

            for i in range(NT + 1):
                if i < NT and sc_inv is not None:
                    sc_inv(s, i)
                if i >= 1:
                    j = i - 1
                    sc_finish(s, j)
                    s.activation(
                        out=xw[j % NXW][:], in_=xw[j % NXW][:], func=AF.Identity,
                        scale=st_A[:, j : j + 1], bias=st_B[:, j : j + 1],
                    ).then_inc(s_t.h, 1)
                    s_t.inc()
                    s.wait_ge(s_t.h, s_t.n)
                    s.dma_start(out=tt[j], in_=xw[j % NXW][:]).then_inc(
                        s_st4[j % NXW].h, 16
                    )
                    s_st4[j % NXW].inc(16)
                    advance(j - 2)
            advance(NT - 1)

        def fA_te(t):
            t.wait_ge(s_const.h, CONSTS)
            if psum_pred:
                for sem, val in psum_pred:
                    t.wait_ge(sem.h, val)
            for i in range(NT):
                t.wait_ge(s_t.h, b["t"] + i + 1)
                if i >= 4:
                    t.wait_ge(s_tev.h, b["tev"] + i - 3)
                pb = (i % 4) * 2
                for c in range(NC8):
                    mm = t.matmul(
                        bank(pb + c // 4, 128 * (c % 4), 128 * (c % 4) + 128),
                        xw[i % NXW][:, 128 * c : 128 * (c + 1)],
                        identF[:],
                        start=(c % 4 == 0), stop=(c % 4 == 3), is_transpose=True,
                    )
                    if c == NC8 - 1:
                        mm.then_inc(s_tp.h, 1)
                s_tp.inc()

        sec(blk.scalar, fA_sc)
        if KF1:
            sec(blk.tensor, fA_te)

        s_ccw = S(f"sccw_{tag}")
        s_ccd = S(f"sccd_{tag}")
        s_ccr = S(f"sccr_{tag}")
        s_rw = S(f"srw_{tag}")
        s_bc = S(f"sbc_{tag}")

        def fC_gp_a(g):
            g.wait_ge(s_mx.h, b["mx"] + 1)
            g.dma_start(out=cc_in[cc_idx][0:1, :], in_=smax[:, 0:1]).then_inc(
                s_ccw.h, 16
            )
            s_ccw.inc(16)
            g.wait_ge(s_ccw.h, s_ccw.n)
            g.collective_compute(
                "AllReduce", OP.max,
                replica_groups=[list(range(n_cores))],
                ins=[cc_in[cc_idx].opt()],
                outs=[cc_out[cc_idx].opt()],
            ).then_inc(s_ccd.h, 1)
            s_ccd.inc()
            g.wait_ge(s_ccd.h, s_ccd.n)
            g.dma_start(out=grow[0:1, :], in_=cc_out[cc_idx][0:1, :]).then_inc(
                s_ccr.h, 16
            )
            s_ccr.inc(16)

        if KF1:
            sec(blk.gpsimd, fC_gp_a)
        ccr_val = s_ccr.n

        s_qs = S(f"sqs_{tag}")
        qs_base = s_qs.n

        def fC_vec1(v):
            v.wait_ge(s_ccr.h, ccr_val)
            if compute_alpha:
                v.wait_ge(s_alf.h, 16)
                ev = strided1(arow.ap()[0:1, 0:1], 2, 128)
                od = strided1(arow.ap()[0:1, 1:2], 2, 128)
                v.reduce_sum(out=s11[0:1, 4:5], in_=ev, axis=AX)
                v.drain()
                v.reduce_sum(out=s11[0:1, 5:6], in_=od, axis=AX)
                v.drain()
                v.tensor_scalar(out=s11[0:1, 6:7], in0=s11[0:1, 5:6], scalar1=1.0,
                                scalar2=None, op0=OP.max)
                v.drain()
                v.reciprocal(out=s11[0:1, 6:7], in_=s11[0:1, 6:7])
                v.drain()
                v.tensor_tensor(out=s11[0:1, 6:7], in0=s11[0:1, 6:7],
                                in1=s11[0:1, 4:5], op=OP.mult)
                v.drain()
            v.reduce_max(out=s11[0:1, 0:1], in_=grow[0:1, :], axis=AX)
            v.drain()
            v.tensor_scalar(
                out=s11[0:1, 1:2], in0=s11[0:1, 0:1],
                scalar1=float(np.float32(1e-8)), scalar2=1.0 / 127.0,
                op0=OP.max, op1=OP.mult,
            )
            v.drain()
            v.reciprocal(out=s11[0:1, 2:3], in_=s11[0:1, 1:2])
            v.drain()
            v.tensor_tensor(out=s11[0:1, 3:4], in0=s11[0:1, 1:2],
                            in1=s11[0:1, 6:7], op=OP.mult)
            v.drain()
            v.tensor_scalar(out=s11[0:1, 3:4], in0=s11[0:1, 3:4], scalar1=0.5,
                            scalar2=None, op0=OP.mult)
            v.drain()
            v.tensor_copy(brow[0:1, 0:128], rep0(s11.ap()[0:1, 2:3], 128))
            v.drain()
            v.tensor_copy(brow[0:1, 128:256], rep0(s11.ap()[0:1, 3:4], 128))
            v.drain().then_inc(s_rw.h, 1)
            s_rw.inc()

        if KF1:
            sec(blk.vector, fC_vec1)

        def fC_gp_b(g):
            g.wait_ge(s_rw.h, s_rw.n)
            g.dma_start(out=invscol[:, 0:1], in_=brow[0:1, 0:128]).then_inc(
                s_bc.h, 16
            )
            s_bc.inc(16)
            g.dma_start(out=hafcol[:, 0:1], in_=brow[0:1, 128:256]).then_inc(
                s_bc.h, 16
            )
            s_bc.inc(16)

        if KF1:
            sec(blk.gpsimd, fC_gp_b)
        bc_val = s_bc.n

        def fC_vec2(v):
            v.wait_ge(s_bc.h, bc_val)
            v.tensor_scalar(out=st_cA[:], in0=st_A[:], scalar1=hafcol[:, 0:1],
                            scalar2=None, op0=OP.mult)
            v.drain()
            v.tensor_scalar(out=gscols[:], in0=gcols[:], scalar1=invscol[:, 0:1],
                            scalar2=None, op0=OP.mult)
            v.drain()
            v.tensor_scalar(
                out=bsccols[:], in0=bcols[:], scalar1=invscol[:, 0:1],
                scalar2=None, op0=OP.mult,
            )
            v.drain().then_inc(s_qs.h, 1)
            s_qs.inc()

        if KF1:
            sec(blk.vector, fC_vec2)

        s_txf = S(f"stxf_{tag}")
        s_tp2 = S(f"stp2_{tag}")
        s_tev2 = S(f"stev2_{tag}")
        s_q8 = S(f"sq8_{tag}")
        s_nb = S(f"snb_{tag}")
        s_fm = S(f"sfm_{tag}")
        s_e1 = S(f"se1_{tag}")
        s_o = S(f"so_{tag}")
        b2 = dict(t=s_t.n, txf=s_txf.n, tp2=s_tp2.n, tev2=s_tev2.n, q8=s_q8.n,
                  nb=s_nb.n, fm=s_fm.n, e1=s_e1.n, o=s_o.n, qs=qs_base,
                  ld=[s_ld[k].n for k in range(NXW)])
        bstB = [s_st4[k].n for k in range(NXW)]

        def fB_sync(sy):
            sy.wait_ge(s_tp.h, s_tp.n)
            for k in range(NXW):
                sy.wait_ge(s_st4[k].h, bstA[k] + 16 * (NT // NXW))
            for i in range(NT):
                if i >= NXW:
                    sy.wait_ge(s_st4[i % NXW].h, bstB[i % NXW] + 16 * (i // NXW))
                    if post_slot_gate is not None:
                        post_slot_gate(sy, i)
                sy.dma_start(out=xw[i % NXW][:], in_=tt[i]).then_inc(
                    s_ld[i % NXW].h, 16
                )
                s_ld[i % NXW].inc(16)

        def fB_te(t):
            def transposes(g):
                for j in range(4):
                    i = 4 * g + j
                    t.wait_ge(s_ld[i % NXW].h, b2["ld"][i % NXW] + 16 * (i // NXW + 1))
                    if g == 0 and j == 0:
                        t.wait_ge(s_tev.h, b["tev"] + NT)
                    if g >= 1 and j == 0:
                        t.wait_ge(s_nb.h, b2["nb"] + g)
                    if i >= 2:
                        t.wait_ge(s_tev2.h, b2["tev2"] + i - 1)
                    pb = (i % 2) * 2
                    for c in range(NC8):
                        mm = t.matmul(
                            bank(pb + c // 4, 128 * (c % 4), 128 * (c % 4) + 128),
                            xw[i % NXW][:, 128 * c : 128 * (c + 1)],
                            identF[:],
                            start=(c % 4 == 0), stop=(c % 4 == 3), is_transpose=True,
                        )
                        if c == NC8 - 1:
                            mm.then_inc(s_tp2.h, 1)
                    s_tp2.inc()

            def ffns(g):
                for j in range(4):
                    i = 4 * g + j
                    t.wait_ge(s_nb.h, b2["nb"] + g + 1)
                    if i == 0:
                        t.wait_ge(s_wsg.h, NC8)
                    if i >= 2:
                        t.wait_ge(s_e1.h, b2["e1"] + i - 1)
                    for oh in range(2):
                        for c in range(NC8):
                            mm = t.matmul(
                                bank(4 + (i % 2) * 2 + oh),
                                nTg[:, 512 * c + 128 * j : 512 * c + 128 * (j + 1)],
                                wsgn[:, c, 512 * oh : 512 * (oh + 1)],
                                start=(c == 0), stop=(c == NC8 - 1),
                            )
                            if oh == 1 and c == NC8 - 1:
                                mm.then_inc(s_fm.h, 1)
                    s_fm.inc()

            transposes(0)
            for g in range(1, NG):
                ffns(g - 1)
                transposes(g)
            ffns(NG - 1)

        def fB_vec(v):
            for g in range(NG):
                v.wait_ge(s_q8.h, b2["q8"] + g + 1)
                v.tensor_scalar(
                    out=nTg[:], in0=tTg[:], scalar1=C_ROUND, scalar2=C_ROUND,
                    op0=OP.add, op1=OP.subtract,
                )
                v.drain().then_inc(s_nb.h, 1)
                s_nb.inc()
                for j in range(4):
                    i = 4 * g + j
                    v.wait_ge(s_fm.h, b2["fm"] + i + 1)
                    v.scalar_tensor_tensor(
                        out=xw[i % NXW][:], in0=bankpair(4 + (i % 2) * 2),
                        scalar=st_cA[:, i : i + 1], in1=xw[i % NXW][:],
                        op0=OP.mult, op1=OP.add,
                    )
                    v.drain().then_inc(s_e1.h, 1)
                    s_e1.inc()
                    if post_vec is not None:
                        post_vec(v, i, b2["t"] + i + 1)
            if post_vec is not None:
                post_vec(v, NT, 0)

        def fB_sc(s):
            s.wait_ge(s_qs.h, b2["qs"] + 1)
            for g in range(NG):
                for j in range(4):
                    i = 4 * g + j
                    s.wait_ge(s_tp2.h, b2["tp2"] + i + 1)
                    s.activation(
                        out=tTg_cw[:, :, 128 * j : 128 * (j + 1)],
                        in_=bankpair((i % 2) * 2), func=AF.Identity,
                    ).then_inc(s_tev2.h, 1)
                    s_tev2.inc()
                for c in range(NC8):
                    act = s.activation(
                        out=tTg[:, 512 * c : 512 * (c + 1)],
                        in_=tTg[:, 512 * c : 512 * (c + 1)],
                        func=AF.Identity,
                        scale=gscols[:, c : c + 1], bias=bsccols[:, c : c + 1],
                    )
                    if c == NC8 - 1:
                        act.then_inc(s_q8.h, 1)
                s_q8.inc()
                for j in range(4):
                    i = 4 * g + j
                    s.wait_ge(s_e1.h, b2["e1"] + i + 1)
                    s.activation(
                        out=xw[i % NXW][:], in_=xw[i % NXW][:], func=AF.Identity,
                        scale=st_invA[:, i : i + 1],
                        bias=st_muv[:, 2 * i : 2 * i + 1],
                    ).then_inc(s_t.h, 1)
                    s_t.inc()
                    s.wait_ge(s_t.h, s_t.n)
                    s.dma_start(out=dst_tiles[i], in_=xw[i % NXW][:]).then_inc(
                        s_st4[i % NXW].h, 16
                    )
                    s_st4[i % NXW].inc(16)
                    if post_sc_group is not None:
                        post_sc_group(s, i)
            if post_sc_group is not None:
                post_sc_group(s, NT)

        if KF2:
            sec(blk.sync, fB_sync)
            sec(blk.tensor, fB_te)
            sec(blk.vector, fB_vec)
            sec(blk.scalar, fB_sc)
        return bstB

    s_bn2 = S("s_bn2")
    s_iv2 = S("s_iv2")
    s_sA2 = S("s_sA2")

    def post_vec1(v, i, tval):
        if i < NT:
            v.wait_ge(S("st_fS").h, tval)
            v.bn_stats(out=bn12[:, 0:6], in_=xw[i % NXW][:, 0:512])
            v.bn_stats(out=bn12[:, 6:12], in_=xw[i % NXW][:, 512:1024])
            v.drain()
            v.bn_aggr(out=s2_muv[:, 2 * i : 2 * i + 2], in_=bn12[:])
            v.drain().then_inc(s_bn2.h, 1)
            s_bn2.inc()
        if i >= 1:
            j = i - 1
            v.wait_ge(s_iv2.h, j + 1)
            v.reciprocal(out=s2_A[:, j : j + 1],
                         in_=stw[3][:, j % 4 : j % 4 + 1])
            v.drain().then_inc(s_sA2.h, 1)
            s_sA2.inc()

    def post_sc1(s, i):
        if i < NT:
            s.wait_ge(s_bn2.h, i + 1)
            vv = s2_muv.ap()[:, 2 * i + 1 : 2 * i + 2]
            s.activation(out=stw[3].ap()[:, i % 4 : i % 4 + 1], in_=vv,
                         func=AF.Sqrt, bias=ccst[:, 0:1]).then_inc(s_iv2.h, 1)
            s_iv2.inc()
        if i >= 1:
            j = i - 1
            s.wait_ge(s_sA2.h, j + 1)
            w2 = stw[2].ap()[:, j % 4 : j % 4 + 1]
            mu = s2_muv.ap()[:, 2 * j : 2 * j + 1]
            s.activation(out=w2, in_=mu, func=AF.Identity, scale=ccst[:, 3:4])
            s.activation(out=s2_B[:, j : j + 1], in_=w2, func=AF.Identity,
                         scale=s2_A.ap()[:, j : j + 1])

    def post_slot1(sy, i):
        sy.wait_ge(s_bn2.h, i - NXW + 1)

    bstX1 = emit_ffn_stage(1, [xt[i] for i in range(NT)],
                           [x1t[i] for i in range(NT)], 0, None, True,
                           post_vec=post_vec1, post_slot_gate=post_slot1,
                           post_sc_group=post_sc1)

    s_wqk = S("s_wqk")

    def wqkv_gp(g):
        g.wait_ge(s_as2.h, NC8)
        for c in range(NC8):
            g.dma_start(
                out=wqkv[:, c, :], in_=wqkvT_h[128 * c : 128 * (c + 1), :]
            ).then_inc(s_wqk.h, 16)
            s_wqk.inc(16)

    sec(blk.gpsimd, wqkv_gp)

    if stages < 2:
        def fin_sync1(sy):
            for k in range(NXW):
                sy.wait_ge(s_st4[k].h, s_st4[k].n)
        sec(blk.sync, fin_sync1)
        stack.close()
        return nc

    s_h2 = S("sh2_a")
    s_h2p = S("sh2p_a")
    s_h2e = S("sh2e_a")
    s_kp = S("skp_a")
    s_kmin = S("skmin_a")
    s_kexp = S("skexp_a")
    s_kev = S("skev_a")
    s_ka = S("ska_a")
    s_kadd = S("skadd_a")
    s_kvb = S("skvb_a")
    aB = dict(h2=s_h2.n, h2p=s_h2p.n, h2e=s_h2e.n, kp=s_kp.n, kmin=s_kmin.n,
              kexp=s_kexp.n, kev=s_kev.n, ka=s_ka.n, kadd=s_kadd.n,
              ld=[s_ld[k].n for k in range(NXW)])
    PRED_2A = [(S("stev2_fS"), S("stev2_fS").n), (S("se1_fS"), S("se1_fS").n)]

    def s2a_sync(sy):
        for i in range(NT):
            if i < NXW:
                sy.wait_ge(s_bn2.h, NT)
                sy.wait_ge(s_st4[i % NXW].h, bstX1[i % NXW] + 16 * (NT // NXW))
            else:
                sy.wait_ge(s_h2.h, aB["h2"] + i - NXW + 1)
                sy.wait_ge(s_st4[i % NXW].h, bstX1[i % NXW] + 16 * (i // NXW + 1))
            sy.dma_start(out=xw[i % NXW][:], in_=x1t[i]).then_inc(
                s_ld[i % NXW].h, 16
            )
            s_ld[i % NXW].inc(16)

    sec(blk.sync, s2a_sync)

    def s2a_sc(s):
        s.wait_ge(s_sA2.h, NT)

        def h2_tile(i):
            s.wait_ge(s_ld[i % NXW].h, aB["ld"][i % NXW] + 16 * (i // NXW + 1))
            if i >= 2:
                s.wait_ge(s_h2p.h, aB["h2p"] + i - 1)
            s.activation(
                out=h2t[i % 2][:], in_=xw[i % NXW][:], func=AF.Identity,
                scale=s2_A[:, i : i + 1], bias=s2_B[:, i : i + 1],
            ).then_inc(s_h2.h, 1)
            s_h2.inc()

        def exp_tile(i):
            for og in range(2):
                s.wait_ge(s_kmin.h, aB["kmin"] + 2 * i + og + 1)
                s.activation(
                    out=scrB[:, 512 * og : 512 * (og + 1)],
                    in_=scrA[:, 512 * og : 512 * (og + 1)], func=AF.Exp,
                ).then_inc(s_kexp.h, 1)
                s_kexp.inc()

        def evac2(j):
            s.wait_ge(s_h2p.h, aB["h2p"] + j + 1)
            if j >= 4:
                s.wait_ge(s_kp.h, aB["kp"] + 4 * (j - 3))
            s.activation(
                out=h2Tt[j % 4][:], in_=bankb16(j % 4), func=AF.Identity,
            ).then_inc(s_h2e.h, 1)
            s_h2e.inc()

        for i in range(NT):
            h2_tile(i)
            if i >= 4:
                exp_tile(i - 4)
            if i >= 1:
                evac2(i - 1)
        evac2(NT - 1)
        for j in range(max(0, NT - 4), NT):
            exp_tile(j)

    sec(blk.scalar, s2a_sc)

    def s2a_te(t):
        def transp_tile(i):
            t.wait_ge(s_h2.h, aB["h2"] + i + 1)
            if i == 0:
                t.wait_ge(s_wqk.h, 16 * NC8)
                for sem, val in PRED_2A:
                    t.wait_ge(sem.h, val)
            if i >= 4:
                t.wait_ge(s_h2e.h, aB["h2e"] + i - 3)
            for c in range(NC8):
                mm = t.matmul(
                    bankb16(i % 4)[:, 128 * c : 128 * (c + 1)],
                    h2t[i % 2][:, 128 * c : 128 * (c + 1)],
                    identB[:],
                    start=(c == 0), stop=(c == NC8 - 1), is_transpose=True,
                )
                if c == NC8 - 1:
                    mm.then_inc(s_h2p.h, 1)
            s_h2p.inc()

        def kv_tile(i):
            t.wait_ge(s_h2e.h, aB["h2e"] + i + 1)
            for og in range(4):
                idx = 4 * i + og
                if og == 0 and i >= 1:
                    t.wait_ge(s_kev.h, aB["kev"] + 3 * i)
                if og == 2:
                    t.wait_ge(s_kev.h, aB["kev"] + 3 * i + 1)
                if og == 3:
                    t.wait_ge(s_kev.h, aB["kev"] + 3 * i + 2)
                for c in range(NC8):
                    mm = t.matmul(
                        bank(4 + og % 2),
                        h2Tt[i % 4][:, 128 * c : 128 * (c + 1)],
                        wqkv[:, c, D + 512 * og : D + 512 * (og + 1)],
                        start=(c == 0), stop=(c == NC8 - 1),
                    )
                    if c == NC8 - 1:
                        mm.then_inc(s_kp.h, 1)
                s_kp.inc()
            t.wait_ge(s_kev.h, aB["kev"] + 3 * (i + 1))
            for q in range(2):
                if 2 * i + q >= 1:
                    t.wait_ge(s_kadd.h, aB["kadd"] + 2 * i + q)
                for c in range(4 * q, 4 * q + 4):
                    cc = c - 4 * q
                    mm = t.matmul(
                        bank(6 + cc // 2, 130 * (cc % 2), 130 * (cc % 2) + 130),
                        ktile[i % 2][:, 128 * c : 128 * (c + 1)],
                        vsb[i % 2][:, 130 * c : 130 * (c + 1)],
                        start=(cc % 2 == 0), stop=(cc % 2 == 1),
                    )
                    if cc == 3:
                        mm.then_inc(s_ka.h, 1)
                s_ka.inc()

        for i in range(min(4, NT)):
            transp_tile(i)
        for i in range(NT - 4):
            kv_tile(i)
            transp_tile(i + 4)
        for i in range(max(0, NT - 4), NT):
            kv_tile(i)

    sec(blk.tensor, s2a_te)

    def s2a_vec(v):
        def kvelem_tile2(i):
            pk2 = PS.ap()[:, 512 * 4 : 512 * 6]
            if i >= 2:
                v.wait_ge(s_ka.h, aB["ka"] + 2 * (i - 1))
            for og in range(2):
                pk = bank(4 + og)
                v.wait_ge(s_kp.h, aB["kp"] + 4 * i + og + 1)
                v.tensor_scalar(
                    out=scrA[:, 512 * og : 512 * (og + 1)], in0=pk,
                    scalar1=0.0, scalar2=None, op0=OP.min,
                )
                v.drain().then_inc(s_kmin.h, 1)
                s_kmin.inc()
                v.wait_ge(s_kexp.h, aB["kexp"] + 2 * i + og + 1)
                v.scalar_tensor_tensor(
                    out=ktile[i % 2][:, 512 * og : 512 * (og + 1)], in0=pk,
                    scalar=0.0, in1=scrB[:, 512 * og : 512 * (og + 1)],
                    op0=OP.max, op1=OP.add,
                )
                v.drain().then_inc(s_kev.h, 1)
                s_kev.inc()
            v.wait_ge(s_kp.h, aB["kp"] + 4 * i + 4)
            dst = vsb[i % 2].ap().rearrange("p (h c) -> p h c", c=65)[
                :, 0:16, 0:64
            ]
            v.tensor_copy(dst, pk2.rearrange("p (h c) -> p h c", c=64))
            v.drain().then_inc(s_kev.h, 1)
            s_kev.inc()
            for q in range(2):
                v.wait_ge(s_ka.h, aB["ka"] + 2 * i + q + 1)
                psrc = PS.ap()[:, 512 * 6 : 512 * 8].rearrange(
                    "p (b s) -> p b s", b=2)[:, :, 0:260]
                kdst = kvf.ap()[:, 520 * q : 520 * (q + 1)].rearrange(
                    "p (b c) -> p b c", b=2)
                v.tensor_tensor(out=kdst, in0=kdst, in1=psrc, op=OP.add)
                v.drain().then_inc(s_kadd.h, 1)
                s_kadd.inc()

        for i in range(NT):
            kvelem_tile2(i)
        kv3 = kvf.ap().rearrange("p (c w) -> p c w", w=130)
        v.memset(kv3[64:128, :, 0:65], 0.0)
        v.drain()
        v.memset(kv3[0:64, :, 65:130], 0.0)
        v.drain()
        v.tensor_copy(kvb[:], kvf[:])
        v.drain().then_inc(s_kvb.h, 1)
        s_kvb.inc()

    sec(blk.vector, s2a_vec)

    s_wpk = S("s_wpk")

    def wproj_gp(g):
        for c in range(NC8):
            g.dma_start(
                out=wproj[:, c, :], in_=wprojT_h[128 * c : 128 * (c + 1), :]
            ).then_inc(s_wpk.h, 16)
            s_wpk.inc(16)

    sec(blk.gpsimd, wproj_gp)

    s_xf2b = S("sxf_s2b")
    s_h2b = S("sh2_b")
    s_h2pb = S("sh2p_b")
    s_h2eb = S("sh2e_b")
    s_qm = S("sqm_b")
    s_qmin = S("sqmin_b")
    s_qexp = S("sqexp_b")
    s_qe = S("sqe_b")
    s_nd = S("snd_b")
    s_dv = S("sdv_b")
    s_ap = S("sap_b")
    s_ae = S("sae_b")
    s_pj = S("spj_b")
    s_rs = S("srs_b")
    s_rr = S("srr_b")
    s_bn3 = S("s_bn3")
    s_iv3 = S("s_iv3")
    s_sA3 = S("s_sA3")
    s_o2 = S("so_s2b")
    bB = dict(xf=s_xf2b.n, h2=s_h2b.n, h2p=s_h2pb.n, h2e=s_h2eb.n,
              qm=s_qm.n, qmin=s_qmin.n, qexp=s_qexp.n, qe=s_qe.n, nd=s_nd.n,
              dv=s_dv.n, ap=s_ap.n, ae=s_ae.n, pj=s_pj.n, rs=s_rs.n, o=s_o2.n,
              rr=s_rr.n,
              ld=[s_ld[k].n for k in range(NXW)])
    bstX2 = [s_st4[k].n for k in range(NXW)]

    def s2b_sync(sy):
        sy.wait_ge(s_h2.h, s_h2.n)
        for i in range(NT):
            if i >= NXW:
                sy.wait_ge(s_st4[i % NXW].h, bstX2[i % NXW] + 16 * (i // NXW))
                sy.wait_ge(s_bn3.h, i - NXW + 1)
            sy.dma_start(out=xw[i % NXW][:], in_=x1t[i]).then_inc(
                s_ld[i % NXW].h, 16
            )
            s_ld[i % NXW].inc(16)

    def s2b_sc(s):
        for g in range(NG):
            for j in range(4):
                i = 4 * g + j
                s.wait_ge(s_ld[i % NXW].h, bB["ld"][i % NXW] + 16 * (i // NXW + 1))
                if i >= 2:
                    s.wait_ge(s_h2pb.h, bB["h2p"] + i - 1)
                s.activation(
                    out=h2t[i % 2][:], in_=xw[i % NXW][:], func=AF.Identity,
                    scale=s2_A[:, i : i + 1], bias=s2_B[:, i : i + 1],
                ).then_inc(s_h2b.h, 1)
                s_h2b.inc()
            for j in range(4):
                i = 4 * g + j
                s.wait_ge(s_h2pb.h, bB["h2p"] + i + 1)
                s.activation(
                    out=h2Tg_cw[:, :, 128 * j : 128 * (j + 1)],
                    in_=bankb16(i % 2), func=AF.Identity,
                ).then_inc(s_h2eb.h, 1)
                s_h2eb.inc()
            for oc in range(NC8):
                idx = NC8 * g + oc
                s.wait_ge(s_qmin.h, bB["qmin"] + idx + 1)
                s.activation(out=scrB[:, 0:512], in_=scrA[:, 0:512],
                             func=AF.Exp).then_inc(s_qexp.h, 1)
                s_qexp.inc()
            for j in range(4):
                i = 4 * g + j
                s.wait_ge(s_ap.h, bB["ap"] + i + 1)
                s.activation(
                    out=h2Tg_cw[:, :, 128 * j : 128 * (j + 1)],
                    in_=bankb16(4), func=AF.Identity,
                ).then_inc(s_ae.h, 1)
                s_ae.inc()
                s.wait_ge(s_rs.h, bB["rs"] + i + 1)
                s.dma_start(out=x2t[i], in_=xw[i % NXW][:]).then_inc(
                    s_st4[i % NXW].h, 16
                )
                s_st4[i % NXW].inc(16)
                s.wait_ge(s_bn3.h, i + 1)
                s.activation(out=st_invA[:, i : i + 1],
                             in_=st_muv[:, 2 * i + 1 : 2 * i + 2], func=AF.Sqrt,
                             scale=ccst[:, 2:3], bias=ccst[:, 1:2]
                             ).then_inc(s_iv3.h, 1)
                s_iv3.inc()
                if i >= 1:
                    s.wait_ge(s_sA3.h, i)
                    s.activation(out=stw[2].ap()[:, (i - 1) % 4 : (i - 1) % 4 + 1],
                                 in_=st_muv[:, 2 * (i - 1) : 2 * i - 1],
                                 func=AF.Identity, scale=ccst[:, 3:4])
                    s.activation(out=st_B[:, i - 1 : i],
                                 in_=stw[2][:, (i - 1) % 4 : (i - 1) % 4 + 1],
                                 func=AF.Identity, scale=st_A.ap()[:, i - 1 : i])
        s.wait_ge(s_sA3.h, NT)
        s.activation(out=stw[2].ap()[:, (NT - 1) % 4 : (NT - 1) % 4 + 1],
                     in_=st_muv[:, 2 * (NT - 1) : 2 * NT - 1],
                     func=AF.Identity, scale=ccst[:, 3:4])
        s.activation(out=st_B[:, NT - 1 : NT],
                     in_=stw[2][:, (NT - 1) % 4 : (NT - 1) % 4 + 1],
                     func=AF.Identity, scale=st_A.ap()[:, NT - 1 : NT])

    PRED_2B = [(s_h2e, s_h2e.n), (s_kev, s_kev.n), (s_kadd, s_kadd.n)]

    def s2b_te(t):
        for g in range(NG):
            for j in range(4):
                i = 4 * g + j
                t.wait_ge(s_h2b.h, bB["h2"] + i + 1)
                if g == 0 and j == 0:
                    for sem, val in PRED_2B:
                        t.wait_ge(sem.h, val)
                if g >= 1 and j == 0:
                    t.wait_ge(s_pj.h, bB["pj"] + 8 * g)
                if i >= 2:
                    t.wait_ge(s_h2eb.h, bB["h2e"] + i - 1)
                for c in range(NC8):
                    mm = t.matmul(
                        bankb16(i % 2)[:, 128 * c : 128 * (c + 1)],
                        h2t[i % 2][:, 128 * c : 128 * (c + 1)],
                        identB[:],
                        start=(c == 0), stop=(c == NC8 - 1), is_transpose=True,
                    )
                    if c == NC8 - 1:
                        mm.then_inc(s_h2pb.h, 1)
                s_h2pb.inc()
            for oc in range(NC8):
                idx = NC8 * g + oc
                t.wait_ge(s_h2eb.h, bB["h2e"] + 4 * (g + 1))
                if idx == 0:
                    t.wait_ge(s_kvb.h, 1)
                if idx >= 2:
                    t.wait_ge(s_qe.h, bB["qe"] + idx - 1)
                for c in range(NC8):
                    mm = t.matmul(
                        bank(2 + oc % 2),
                        wqkv[:, c, 128 * oc : 128 * (oc + 1)],
                        h2Tg[:, 512 * c : 512 * (c + 1)],
                        start=(c == 0), stop=(c == NC8 - 1),
                    )
                    if c == NC8 - 1:
                        mm.then_inc(s_qm.h, 1)
                s_qm.inc()
                t.wait_ge(s_qe.h, bB["qe"] + idx + 1)
                if g >= 1 and oc == 0:
                    t.wait_ge(s_ae.h, bB["ae"] + 4 * g)
                    t.wait_ge(s_rs.h, bB["rs"] + 4 * g)
                for ts in range(4):
                    nd_idx = idx * 4 + ts
                    if nd_idx >= 4:
                        t.wait_ge(s_dv.h, bB["dv"] + nd_idx - 3)
                    t.matmul(
                        bank(4 + ts, 0, 130),
                        qg[:, 512 * oc + 128 * ts : 512 * oc + 128 * (ts + 1)],
                        kvb[:, 130 * oc : 130 * (oc + 1)],
                        start=True, stop=True,
                    ).then_inc(s_nd.h, 1)
                    s_nd.inc()
            for j in range(4):
                i = 4 * g + j
                t.wait_ge(s_dv.h, bB["dv"] + 32 * (g + 1))
                if i >= 1:
                    t.wait_ge(s_ae.h, bB["ae"] + i)
                for c in range(NC8):
                    mm = t.matmul(
                        bankb16(4)[:, 128 * c : 128 * (c + 1)],
                        attng_ap[:, 1024 * j + 128 * c : 1024 * j + 128 * (c + 1)],
                        identB[:],
                        start=(c == 0), stop=(c == NC8 - 1), is_transpose=True,
                    )
                    if c == NC8 - 1:
                        mm.then_inc(s_ap.h, 1)
                s_ap.inc()
                t.wait_ge(s_ae.h, bB["ae"] + i + 1)
                if i == 0:
                    t.wait_ge(s_wpk.h, 16 * NC8)
                    t.wait_ge(s_bpj.h, 16)
                for oh in range(2):
                    pidx = 2 * i + oh
                    if pidx >= 2:
                        t.wait_ge(s_rs.h, bB["rs"] + pidx // 2)
                    for c in range(NC8):
                        t.matmul(
                            bank(5 + oh),
                            h2Tg[:, 512 * c + 128 * j : 512 * c + 128 * (j + 1)],
                            wproj[:, c, 512 * oh : 512 * (oh + 1)],
                            start=(c == 0), stop=False,
                        )
                    t.matmul(
                        bank(5 + oh),
                        ones1[:, :],
                        bprojb[0:1, 512 * oh : 512 * (oh + 1)],
                        start=False, stop=True,
                    ).then_inc(s_pj.h, 1)
                    s_pj.inc()

    def s2b_vec(v):
        _a = PS.ap()[:, 512 * 4 + 64 : 512 * 4 + 65]
        nd_dencols = bass.AP(tensor=_a.tensor, offset=_a.offset,
                             ap=[list(_a.ap[0]), [512, 4], [65, 2]])
        rcb8 = rcb.ap()[:, 0:8].rearrange("p (b h) -> p b h", b=4)
        for g in range(NG):
            for oc in range(NC8):
                idx = NC8 * g + oc
                v.wait_ge(s_qm.h, bB["qm"] + idx + 1)
                pq = bank(2 + oc % 2)
                v.tensor_scalar(out=scrA[:, 0:512], in0=pq, scalar1=0.0,
                                scalar2=None, op0=OP.min)
                v.drain().then_inc(s_qmin.h, 1)
                s_qmin.inc()
                v.wait_ge(s_qexp.h, bB["qexp"] + idx + 1)
                v.scalar_tensor_tensor(
                    out=qg[:, 512 * oc : 512 * (oc + 1)], in0=pq, scalar=0.0,
                    in1=scrB[:, 0:512], op0=OP.max, op1=OP.add,
                )
                v.drain().then_inc(s_qe.h, 1)
                s_qe.inc()
                v.wait_ge(s_nd.h, bB["nd"] + 4 * idx + 4)
                v.tensor_scalar(out=rcb8, in0=nd_dencols,
                                scalar1=float(np.float32(1e-8)),
                                scalar2=None, op0=OP.add)
                v.drain()
                v.reciprocal(out=rrb[:, 0:8], in_=rcb[:, 0:8])
                v.drain()
                for ts in range(4):
                    ndap = PS.ap()[:, 512 * (4 + ts) : 512 * (4 + ts) + 130]
                    for hh in range(2):
                        v.tensor_scalar(
                            out=attng_ap[:, 1024 * ts + 128 * oc + 64 * hh :
                                         1024 * ts + 128 * oc + 64 * (hh + 1)],
                            in0=ndap[:, 65 * hh : 65 * hh + 64],
                            scalar1=rrb[:, 2 * ts + hh : 2 * ts + hh + 1],
                            scalar2=None, op0=OP.mult,
                        )
                v.drain().then_inc(s_dv.h, 4)
                s_dv.inc(4)
            for j in range(4):
                i = 4 * g + j
                v.wait_ge(s_pj.h, bB["pj"] + 2 * i + 2)
                v.scalar_tensor_tensor(
                    out=xw[i % NXW][:], in0=PS.ap()[:, 512 * 5 : 512 * 7],
                    scalar=0.0, in1=xw[i % NXW][:],
                    op0=OP.bypass, op1=OP.add,
                )
                v.drain().then_inc(s_rs.h, 1)
                s_rs.inc()
                v.bn_stats(out=bn12[:, 0:6], in_=xw[i % NXW][:, 0:512])
                v.bn_stats(out=bn12[:, 6:12], in_=xw[i % NXW][:, 512:1024])
                v.drain()
                v.bn_aggr(out=st_muv[:, 2 * i : 2 * i + 2], in_=bn12[:])
                v.drain().then_inc(s_bn3.h, 1)
                s_bn3.inc()
                if i >= 1:
                    v.wait_ge(s_iv3.h, i)
                    v.reciprocal(out=st_A[:, i - 1 : i], in_=st_invA[:, i - 1 : i])
                    v.drain().then_inc(s_sA3.h, 1)
                    s_sA3.inc()
        v.wait_ge(s_iv3.h, NT)
        v.reciprocal(out=st_A[:, NT - 1 : NT], in_=st_invA[:, NT - 1 : NT])
        v.drain().then_inc(s_sA3.h, 1)
        s_sA3.inc()

    sec(blk.sync, s2b_sync)
    sec(blk.scalar, s2b_sc)
    sec(blk.tensor, s2b_te)
    sec(blk.vector, s2b_vec)

    if stages < 3:
        def fin_sync2(sy):
            for k in range(NXW):
                sy.wait_ge(s_st4[k].h, s_st4[k].n)
        sec(blk.sync, fin_sync2)
        stack.close()
        return nc

    def gate_x2(sy, i):
        if i < NXW:
            sy.wait_ge(s_st4[i % NXW].h, bstX2[i % NXW] + 16 * (NT // NXW))
            sy.wait_ge(s_bn3.h, NT - NXW + i + 1)
        else:
            sy.wait_ge(s_st4[i % NXW].h, bstX2[i % NXW] + 16 * (i // NXW + 1))

    emit_ffn_stage(3, [x2t[i] for i in range(NT)],
                   [outt[i] for i in range(NT)], 1, gate_x2, False,
                   psum_pred=[(s_rs, s_rs.n), (s_dv, s_dv.n), (s_ae, s_ae.n),
                              (s_qe, s_qe.n), (s_h2eb, s_h2eb.n)],
                   pre_stats=(s_sA3, 0))

    def fin_sync(sy):
        for k in range(NXW):
            sy.wait_ge(s_st4[k].h, s_st4[k].n)

    sec(blk.sync, fin_sync)

    stack.close()
    return nc



def _prep_inputs(L0_attn_x, L0_attn_w_qkv, L0_attn_w_proj, L0_attn_b_proj,
                 L0_ffn_weight, L0_ffn_gamma, L0_ffn_beta):
    wqkvT = np.ascontiguousarray(np.asarray(L0_attn_w_qkv).T).astype(np.float32)
    wprojT = np.ascontiguousarray(np.asarray(L0_attn_w_proj).T).astype(np.float32)
    wffnT = np.ascontiguousarray(np.asarray(L0_ffn_weight).T).astype(np.float32)
    gcols = np.ascontiguousarray(
        np.asarray(L0_ffn_gamma).astype(np.float32).reshape(NC8, 128).T)
    bcols = np.ascontiguousarray(
        np.asarray(L0_ffn_beta).astype(np.float32).reshape(NC8, 128).T)
    bprojr = np.ascontiguousarray(
        np.asarray(L0_attn_b_proj).astype(np.float32).reshape(1, D))
    shared = dict(
        wqkvT=wqkvT, wprojT=wprojT, wffnT=wffnT, gcolsi=gcols, bcolsi=bcols,
        bprojr=bprojr,
        identF=np.eye(128, dtype=np.float32),
        identB=np.eye(128, dtype=ml_dtypes.bfloat16),
        ones1=np.ones((1, 128), dtype=ml_dtypes.bfloat16),
    )
    xs = np.asarray(L0_attn_x)
    in_maps = []
    for bidx in range(xs.shape[0]):
        m = dict(shared)
        m["x_in"] = np.ascontiguousarray(xs[bidx].astype(np.float32))
        in_maps.append(m)
    return in_maps


def kernel(L0_attn_x, L0_attn_w_qkv, L0_attn_w_proj, L0_attn_b_proj,
           L0_ffn_weight, L0_ffn_gamma, L0_ffn_beta):
    n_cores = 8
    in_maps = _prep_inputs(L0_attn_x, L0_attn_w_qkv, L0_attn_w_proj,
                           L0_attn_b_proj, L0_ffn_weight, L0_ffn_gamma,
                           L0_ffn_beta)
    nc = build_nc(T=4096, n_cores=n_cores)
    res = run_bass_kernel_spmd(nc, in_maps, core_ids=list(range(n_cores)))
    out = np.stack([np.asarray(r["out"]) for r in res.results], axis=0)
    return out.astype(np.float32)


# version 14
# speedup vs baseline: 1.0507x; 1.0507x over previous
from contextlib import ExitStack

import numpy as np
import ml_dtypes

import concourse.bass as bass
import concourse.mybir as mybir
from concourse.bass_utils import run_bass_kernel_spmd

F32 = mybir.dt.float32
BF16 = mybir.dt.bfloat16
AX = mybir.AxisListType.X
OP = mybir.AluOpType
AF = mybir.ActivationFunctionType

D = 1024
NC8 = 8
HEADS = 16
C_ROUND = 12582912.0
EPS = float(np.float32(1e-5))
F = np.float32


class Ctr:
    def __init__(self, stack, nc, name):
        self.h = stack.enter_context(nc.semaphore(name))
        self.n = 0

    def inc(self, k=1):
        self.n += k
        return self.n


def build_nc(T=4096, n_cores=8, stages=3):
    NT = T // 128
    NG = NT // 4
    nc = bass.Bass("TRN2", target_bir_lowering=False, debug=False)

    x_in = nc.dram_tensor("x_in", [T, D], F32, kind="ExternalInput").ap()
    wqkvT_h = nc.dram_tensor("wqkvT", [D, 3 * D], F32, kind="ExternalInput").ap()
    wprojT_h = nc.dram_tensor("wprojT", [D, D], F32, kind="ExternalInput").ap()
    wffnT_h = nc.dram_tensor("wffnT", [D, D], F32, kind="ExternalInput").ap()
    gcols_h = nc.dram_tensor("gcolsi", [128, NC8], F32, kind="ExternalInput").ap()
    bcols_h = nc.dram_tensor("bcolsi", [128, NC8], F32, kind="ExternalInput").ap()
    bprojr_h = nc.dram_tensor("bprojr", [1, D], F32, kind="ExternalInput").ap()
    identF_h = nc.dram_tensor("identF", [128, 128], F32, kind="ExternalInput").ap()
    identB_h = nc.dram_tensor("identB", [128, 128], BF16, kind="ExternalInput").ap()
    ones1_h = nc.dram_tensor("ones1", [1, 128], BF16, kind="ExternalInput").ap()
    out_h = nc.dram_tensor("out", [T, D], F32, kind="ExternalOutput").ap()

    t_d = nc.dram_tensor("t_scr", [T, D], F32).ap()
    x1_d = nc.dram_tensor("x1_scr", [T, D], F32).ap()
    x2_d = nc.dram_tensor("x2_scr", [T, D], F32).ap()
    cc_in = [nc.dram_tensor(f"cc_in{i}", [1, 128], F32).ap() for i in range(2)]
    cc_out = [nc.dram_tensor(f"cc_out{i}", [1, 128], F32).ap() for i in range(2)]

    xt = x_in.rearrange("(n p) d -> n p d", p=128)
    tt = t_d.rearrange("(n p) d -> n p d", p=128)
    x1t = x1_d.rearrange("(n p) d -> n p d", p=128)
    x2t = x2_d.rearrange("(n p) d -> n p d", p=128)
    outt = out_h.rearrange("(n p) d -> n p d", p=128)

    stack = ExitStack()
    e = stack.enter_context

    NXW = 8
    xw = [e(nc.sbuf_tensor(f"xw{i}", [128, D], F32)) for i in range(NXW)]
    tTg = e(nc.sbuf_tensor("tTg", [128, 4 * D], F32))
    nTg = e(nc.sbuf_tensor("nTg", [128, 4 * D], BF16))
    wsgn = e(nc.sbuf_tensor("wsgn", [128, NC8, D], BF16))
    wqkv = e(nc.sbuf_tensor("wqkv", [128, NC8, 3 * D], BF16))
    wproj = e(nc.sbuf_tensor("wproj", [128, NC8, D], BF16))
    scrA = e(nc.sbuf_tensor("scrA", [128, D], F32))
    scrB = e(nc.sbuf_tensor("scrB", [128, D], F32))
    h2t = [e(nc.sbuf_tensor(f"h2t{i}", [128, D], BF16)) for i in range(2)]
    h2Tt = [e(nc.sbuf_tensor(f"h2Tt{i}", [128, D], BF16)) for i in range(4)]
    h2Tg = e(nc.sbuf_tensor("h2Tg", [128, 4 * D], BF16))

    ktile = [e(nc.sbuf_tensor(f"ktile{i}", [128, D], BF16)) for i in range(2)]
    vsb = [e(nc.sbuf_tensor(f"vsb{i}", [128, HEADS * 65], BF16)) for i in range(2)]
    kvf = e(nc.sbuf_tensor("kvf", [128, NC8 * 130], F32))
    kvb = e(nc.sbuf_tensor("kvb", [128, NC8 * 130], BF16))
    st_muv = e(nc.sbuf_tensor("st_muv", [128, 2 * NT], F32))
    st_A = e(nc.sbuf_tensor("st_A", [128, NT], F32))
    st_B = e(nc.sbuf_tensor("st_B", [128, NT], F32))
    st_invA = e(nc.sbuf_tensor("st_invA", [128, NT], F32))
    st_cA = e(nc.sbuf_tensor("st_cA", [128, NT], F32))
    s2_muv = e(nc.sbuf_tensor("s2_muv", [128, 2 * NT], F32))
    s2_A = e(nc.sbuf_tensor("s2_A", [128, NT], F32))
    s2_B = e(nc.sbuf_tensor("s2_B", [128, NT], F32))
    stw = [e(nc.sbuf_tensor(f"stw{i}", [128, 4], F32)) for i in range(5)]
    bn12 = e(nc.sbuf_tensor("bn12", [128, 12], F32))
    maxcol = e(nc.sbuf_tensor("maxcol", [128, max(NG, 1)], F32))
    smax = e(nc.sbuf_tensor("smax", [128, 1], F32))
    grow = e(nc.sbuf_tensor("grow", [1, 128], F32))
    arow = e(nc.sbuf_tensor("arow", [1, 256], F32))
    drow = e(nc.sbuf_tensor("drow", [1, 128], F32))
    brow = e(nc.sbuf_tensor("brow", [1, 256], F32))
    s11 = e(nc.sbuf_tensor("s11", [1, 8], F32))
    invscol = e(nc.sbuf_tensor("invscol", [128, 1], F32))
    hafcol = e(nc.sbuf_tensor("hafcol", [128, 1], F32))
    gscols = e(nc.sbuf_tensor("gscols", [128, NC8], F32))
    bsccols = e(nc.sbuf_tensor("bsccols", [128, NC8], F32))
    gcols = e(nc.sbuf_tensor("gcols", [128, NC8], F32))
    bcols = e(nc.sbuf_tensor("bcols", [128, NC8], F32))
    sumabs = e(nc.sbuf_tensor("sumabs", [128, NC8], F32))
    cntc = e(nc.sbuf_tensor("cntc", [128, NC8], F32))
    msumc = e(nc.sbuf_tensor("msumc", [128, NC8], F32))
    al2 = e(nc.sbuf_tensor("al2", [128, 2], F32))
    deltac = e(nc.sbuf_tensor("deltac", [128, 1], F32))
    identF = e(nc.sbuf_tensor("identFs", [128, 128], F32))
    identB = e(nc.sbuf_tensor("identBs", [128, 128], BF16))
    ones1 = e(nc.sbuf_tensor("ones1s", [1, 128], BF16))
    bprojb = e(nc.sbuf_tensor("bprojb", [1, D], BF16))
    rcb = e(nc.sbuf_tensor("rcb", [128, 8], F32))
    rrb = e(nc.sbuf_tensor("rrb", [128, 8], F32))
    ccst = e(nc.sbuf_tensor("ccst", [128, 4], F32))

    qg = nTg
    attng_ap = tTg.ap().bitcast(BF16)
    wstage_ap = wqkv.ap().rearrange("p a b -> p (a b)").bitcast(F32)
    scrC_ap = nTg.ap().bitcast(F32)

    PS = e(nc.psum_tensor("PS", [128, 4096], F32))

    def bank(b, lo=0, hi=512):
        return PS[:, 512 * b + lo : 512 * b + hi]

    def bankpair(b):
        return PS.ap()[:, 512 * b : 512 * b + 1024]

    def bankb16(b):
        return PS.ap()[:, 512 * b : 512 * (b + 1)].bitcast(BF16)

    def rep0(src11, n):
        a = src11
        return bass.AP(tensor=a.tensor, offset=a.offset,
                       ap=[list(a.ap[0]), [0, n]])

    def strided1(src, step, count):
        a = src
        return bass.AP(tensor=a.tensor, offset=a.offset,
                       ap=[list(a.ap[0]), [step, count]])

    sems = {}
    by_handle = {}
    blk = e(nc.Block())

    def S(name):
        if name not in sems:
            c = Ctr(stack, nc, name)
            sems[name] = c
            by_handle[id(c.h)] = c
        return sems[name]

    _CHAIN_OPS = {
        "tensor_scalar", "tensor_tensor", "scalar_tensor_tensor", "tensor_copy",
        "reciprocal", "reduce_max", "reduce_sum", "bn_stats", "bn_aggr",
        "memset", "tensor_scalar_add", "tensor_scalar_mul", "activation",
        "copy",
    }
    chain_state = {}

    class _Shim:
        __slots__ = ("host", "ins")

        def __init__(self, host, ins):
            self.host = host
            self.ins = ins

        def then_inc(self, semh, k):
            self.host._attach(self.ins, semh, k)
            return self.ins

    class ChainEng:
        def __init__(self, eng, key):
            self._e = eng
            self._key = key
            if key not in chain_state:
                chain_state[key] = {"last": None, "pending": None,
                                    "ctr": S(f"chain_{key}")}

        def _flush(self):
            st = chain_state[self._key]
            if st["pending"] is not None:
                ctr = st["ctr"]
                st["pending"].then_inc(ctr.h, 1)
                ctr.inc()
                st["last"] = (ctr.h, ctr.n)
                st["pending"] = None

        def _attach(self, ins, semh, k):
            st = chain_state[self._key]
            ctr = by_handle[id(semh)]
            ins.then_inc(semh, k)
            st["last"] = (semh, ctr.n + k)
            if st["pending"] is ins:
                st["pending"] = None

        def wait_ge(self, h, v):
            self._e.wait_ge(h, v)

        def drain(self):
            st = chain_state[self._key]
            if st["pending"] is not None:
                return _Shim(self, st["pending"])
            ins = self._e.drain()
            st["pending"] = ins
            return _Shim(self, ins)

        def __getattr__(self, name):
            raw = getattr(self._e, name)
            if name not in _CHAIN_OPS:
                return raw

            def emit(*a, **kw):
                self._flush()
                st = chain_state[self._key]
                if st["last"] is not None:
                    self._e.wait_ge(st["last"][0], st["last"][1])
                ins = raw(*a, **kw)
                st["pending"] = ins
                return _Shim(self, ins)

            return emit

    import os
    _seclim = int(os.environ.get("KSEC_LIMIT", "9999"))
    KF1 = os.environ.get("KF1", "1") == "1"
    KF2 = os.environ.get("KF2", "1") == "1"
    _secn = [0]

    def sec(dec, fn):
        _secn[0] += 1
        if _secn[0] <= _seclim:
            if getattr(dec, "__name__", "") == "scalar":
                dec(lambda e2: fn(ChainEng(e2, "s")))
            else:
                dec(fn)

    s_ld = [S(f"s_ld{k}") for k in range(NXW)]
    s_st4 = [S(f"s_st{k}") for k in range(NXW)]
    s_ws = [S(f"s_ws{k}") for k in range(2)]

    wst = wstage_ap[:, 0:2048].rearrange("p (s k) -> p s k", s=2)
    tTg_cw = tTg.ap().rearrange("p (c w) -> p c w", c=NC8)
    h2Tg_cw = h2Tg.ap().rearrange("p (c w) -> p c w", c=NC8)

    s_const = S("s_const")

    def ph0_sync(sy):
        for dst, src in [
            (gcols[:], gcols_h[:, :]), (bcols[:], bcols_h[:, :]),
            (identF[:], identF_h[:, :]), (identB[:], identB_h[:, :]),
            (ones1[:], ones1_h[:, :]),
        ]:
            sy.dma_start(out=dst, in_=src).then_inc(s_const.h, 16)
            s_const.inc(16)

    sec(blk.sync, ph0_sync)
    CONSTS = s_const.n

    s_zr = S("s_zr")

    def ph0_vec(v):
        v.memset(kvf[:], 0.0)
        v.memset(ccst[:, 0:1], EPS)
        v.memset(ccst[:, 1:2], float(np.float32(EPS) * np.float32(EPS)))
        v.memset(ccst[:, 2:3], float(np.float32(1.0) + np.float32(EPS)))
        v.memset(ccst[:, 3:4], -1.0)
        for i in range(2):
            v.memset(vsb[i].ap().rearrange("p (h c) -> p h c", c=65)[:, :, 64:65], 1.0)
        v.drain().then_inc(s_zr.h, 1)
        s_zr.inc()

    sec(blk.vector, ph0_vec)

    s_bpj = S("s_bpj")

    def ph0_gp(g):
        g.dma_start(out=bprojb[:], in_=bprojr_h[:, :]).then_inc(s_bpj.h, 16)
        s_bpj.inc(16)

    sec(blk.gpsimd, ph0_gp)

    s_wf1 = S("s_wf1")
    s_wf1u = S("s_wf1u")
    s_wf2 = S("s_wf2")
    s_wf2u = S("s_wf2u")
    s_abs1 = S("s_abs1")
    s_dl1 = S("s_dl1")
    s_dl2 = S("s_dl2")
    s_dl3 = S("s_dl3")
    s_dl4 = S("s_dl4")
    s_as2 = S("s_as2")
    s_msk = S("s_msk")
    s_wsg = S("s_wsg")
    s_al1 = S("s_al1")
    s_alf = S("s_alf")

    def wffn_gp(g):
        for c in range(NC8):
            if c >= 2:
                g.wait_ge(s_abs1.h, c - 1)
            g.dma_start(
                out=wst[:, c % 2, :], in_=wffnT_h[128 * c : 128 * (c + 1), :]
            ).then_inc(s_ws[c % 2].h, 16)
            s_ws[c % 2].inc(16)
        g.wait_ge(s_dl1.h, 1)
        g.dma_start(out=drow[0:1, :], in_=deltac[:, 0:1]).then_inc(s_dl2.h, 16)
        s_dl2.inc(16)
        g.wait_ge(s_dl3.h, 1)
        g.dma_start(out=deltac[:, 0:1], in_=rep0(drow.ap()[0:1, 0:1], 128)).then_inc(
            s_dl4.h, 16
        )
        s_dl4.inc(16)
        for c in range(NC8):
            if c == 0:
                g.wait_ge(s_abs1.h, NC8)
            if c >= 2:
                g.wait_ge(s_as2.h, c - 1)
            g.dma_start(
                out=wst[:, c % 2, :], in_=wffnT_h[128 * c : 128 * (c + 1), :]
            ).then_inc(s_ws[c % 2].h, 16)
            s_ws[c % 2].inc(16)
        g.wait_ge(s_al1.h, 1)
        g.dma_start(out=arow[0:1, :], in_=al2[:, :]).then_inc(s_alf.h, 16)
        s_alf.inc(16)

    sec(blk.gpsimd, wffn_gp)

    def wffn_sc(s):
        ws_base = [16 * 4 * 0, 0]
        for c in range(NC8):
            s.wait_ge(s_ws[c % 2].h, 16 * (c // 2 + 1))
            s.activation(
                out=scrA[:], in_=wst[:, c % 2, :], func=AF.Abs,
                accum_out=sumabs[:, c : c + 1],
            ).then_inc(s_abs1.h, 1)
            s_abs1.inc()
            s.drain()
        for c in range(NC8):
            s.wait_ge(s_ws[c % 2].h, 16 * 4 + 16 * (c // 2 + 1))
            if c >= 1:
                s.wait_ge(s_msk.h, c)
            s.activation(out=scrA[:], in_=wst[:, c % 2, :], func=AF.Abs)
            s.activation(out=scrB[:], in_=wst[:, c % 2, :], func=AF.Sign).then_inc(
                s_as2.h, 1
            )
            s_as2.inc()
            s.nop().then_inc(s_wf2u.h, 1)
            s_wf2u.inc()

    sec(blk.scalar, wffn_sc)

    def wffn_vec(v):
        v.wait_ge(s_abs1.h, NC8)
        v.reduce_sum(out=deltac[:], in_=sumabs[:], axis=AX)
        v.drain().then_inc(s_dl1.h, 1)
        s_dl1.inc()
        v.wait_ge(s_dl2.h, 16)
        v.reduce_sum(out=s11[0:1, 7:8], in_=drow[0:1, :], axis=AX)
        v.drain()
        v.tensor_scalar(
            out=drow[0:1, 0:1], in0=s11[0:1, 7:8],
            scalar1=float(np.float32(0.7)) / (D * D), scalar2=None, op0=OP.mult,
        )
        v.drain().then_inc(s_dl3.h, 1)
        s_dl3.inc()
        for c in range(NC8):
            v.wait_ge(s_as2.h, c + 1)
            if c == 0:
                v.wait_ge(s_dl4.h, 16)
            v.scalar_tensor_tensor(
                out=scrC_ap[:, 0:D], in0=scrA[:], scalar=deltac[:, 0:1], in1=scrA[:],
                op0=OP.is_gt, op1=OP.mult, accum_out=msumc[:, c : c + 1],
            )
            v.drain()
            v.tensor_scalar(
                out=scrC_ap[:, 0:D], in0=scrA[:], scalar1=deltac[:, 0:1], scalar2=0.0,
                op0=OP.is_gt, op1=OP.add, accum_out=cntc[:, c : c + 1],
            )
            v.drain()
            v.tensor_tensor(out=wsgn[:, c, :], in0=scrC_ap[:, 0:D], in1=scrB[:], op=OP.mult)
            v.drain().then_inc(s_wsg.h, 1)
            s_wsg.inc()
            v.drain().then_inc(s_msk.h, 1)
            s_msk.inc()
        v.reduce_sum(out=al2[:, 0:1], in_=msumc[:], axis=AX)
        v.drain()
        v.reduce_sum(out=al2[:, 1:2], in_=cntc[:], axis=AX)
        v.drain().then_inc(s_al1.h, 1)
        s_al1.inc()

    sec(blk.vector, wffn_vec)

    def emit_bn(tag, src_tiles, muv, sA, sB, sInvA, two_r, gate=None,
                slot_gate=None, vec_tile=None, vec_tail=None):
        s_bn = S(f"sbn_{tag}")
        bases = dict(bn=s_bn.n, ld=[s_ld[k].n for k in range(NXW)])

        def f_sync(sy):
            for i in range(NT):
                if gate is not None:
                    gate(sy, i)
                if i >= NXW and slot_gate is not None:
                    slot_gate(sy, i)
                sy.dma_start(out=xw[i % NXW][:], in_=src_tiles[i]).then_inc(
                    s_ld[i % NXW].h, 16
                )
                s_ld[i % NXW].inc(16)

        s_iv = S(f"siv_{tag}")
        s_stA = S(f"sstA_{tag}")
        bases["iv"] = s_iv.n
        bases["stA"] = s_stA.n

        def recip_tile(v, j):
            v.wait_ge(s_iv.h, bases["iv"] + j + 1)
            v.reciprocal(out=sA[:, j : j + 1], in_=sInvA[:, j : j + 1])
            v.drain().then_inc(s_stA.h, 1)
            s_stA.inc()

        def f_vec(v):
            for i in range(NT):
                v.wait_ge(s_ld[i % NXW].h, bases["ld"][i % NXW] + 16 * (i // NXW + 1))
                v.bn_stats(out=bn12[:, 0:6], in_=xw[i % NXW][:, 0:512])
                v.bn_stats(out=bn12[:, 6:12], in_=xw[i % NXW][:, 512:1024])
                v.drain()
                v.bn_aggr(out=muv[:, 2 * i : 2 * i + 2], in_=bn12[:])
                v.drain().then_inc(s_bn.h, 1)
                s_bn.inc()
                if i >= 1:
                    recip_tile(v, i - 1)
                if vec_tile is not None:
                    vec_tile(v, i)
            recip_tile(v, NT - 1)
            if vec_tail is not None:
                vec_tail(v)

        sec(blk.sync, f_sync)
        sec(blk.vector, f_vec)

        def sc_inv(s, i):
            s.wait_ge(s_bn.h, bases["bn"] + i + 1)
            vv = muv.ap()[:, 2 * i + 1 : 2 * i + 2]
            if two_r:
                s.activation(out=sInvA[:, i : i + 1], in_=vv, func=AF.Sqrt,
                             scale=ccst[:, 2:3], bias=ccst[:, 1:2]
                             ).then_inc(s_iv.h, 1)
            else:
                s.activation(out=sInvA[:, i : i + 1], in_=vv, func=AF.Sqrt,
                             bias=ccst[:, 0:1]).then_inc(s_iv.h, 1)
            s_iv.inc()

        def sc_finish(s, j):
            s.wait_ge(s_stA.h, bases["stA"] + j + 1)
            w2 = stw[2].ap()[:, j % 4 : j % 4 + 1]
            mu = muv.ap()[:, 2 * j : 2 * j + 1]
            s.activation(out=w2, in_=mu, func=AF.Identity, scale=ccst[:, 3:4])
            s.activation(out=sB[:, j : j + 1], in_=w2, func=AF.Identity,
                         scale=sA.ap()[:, j : j + 1])

        return bases, sc_inv, sc_finish

    def emit_ffn_stage(stage_no, src_tiles, dst_tiles, cc_idx, gate, compute_alpha,
                       psum_pred=None, post_vec=None, post_slot_gate=None,
                       post_sc_group=None):
        tag = "fS"
        bstA = [s_st4[k].n for k in range(NXW)]
        s_t = S(f"st_{tag}")
        s_tp = S(f"stp_{tag}")
        s_tev = S(f"stev_{tag}")
        s_xn = S(f"sxn_{tag}")
        s_red = S(f"sred_{tag}")
        s_mx = S(f"smx_{tag}")
        b = dict(t=s_t.n, tp=s_tp.n, tev=s_tev.n, xn=s_xn.n,
                 red=s_red.n, mx=s_mx.n)

        def slot_gate(sy, i):
            if KF1:
                sy.wait_ge(s_tp.h, b["tp"] + i - NXW + 1)
            sy.wait_ge(s_st4[i % NXW].h, bstA[i % NXW] + 16 * (i // NXW))

        def red_group(v, g):
            v.wait_ge(s_xn.h, b["xn"] + g + 1)
            v.reduce_max(
                out=maxcol[:, g : g + 1], in_=tTg[:],
                axis=AX, apply_absolute_value=True,
            )
            v.drain().then_inc(s_red.h, 1)
            s_red.inc()

        ngdone = [0]

        def vec_tile_f(v, i):
            if not KF1:
                return
            if i >= 6 and (i - 6) % 4 == 0:
                red_group(v, (i - 6) // 4)
                ngdone[0] += 1

        def vec_tail(v):
            if KF1:
                for g in range(ngdone[0], NG):
                    red_group(v, g)
                v.reduce_max(out=smax[:], in_=maxcol[:], axis=AX,
                             apply_absolute_value=True)
                v.drain().then_inc(s_mx.h, 1)
                s_mx.inc()

        bnb, sc_inv, sc_finish = emit_bn(
            tag + "a", src_tiles, st_muv, st_A, st_B, st_invA, True, gate,
            slot_gate=slot_gate, vec_tile=vec_tile_f, vec_tail=vec_tail,
        )

        def fA_sc(s):
            s.wait_ge(s_const.h, CONSTS)
            s.wait_ge(s_zr.h, 1)

            def xn_block(g):
                for c in range(NC8):
                    act = s.activation(
                        out=tTg[:, 512 * c : 512 * (c + 1)],
                        in_=tTg[:, 512 * c : 512 * (c + 1)],
                        func=AF.Identity,
                        scale=gcols[:, c : c + 1], bias=bcols[:, c : c + 1],
                    )
                    if c == NC8 - 1:
                        act.then_inc(s_xn.h, 1)
                s_xn.inc()

            def evac(j):
                if j % 4 == 0 and j >= 4:
                    s.wait_ge(s_red.h, b["red"] + j // 4)
                s.wait_ge(s_tp.h, b["tp"] + j + 1)
                s.activation(
                    out=tTg_cw[:, :, 128 * (j % 4) : 128 * (j % 4 + 1)],
                    in_=bankpair((j % 4) * 2), func=AF.Identity,
                ).then_inc(s_tev.h, 1)
                s_tev.inc()
                if j % 4 == 3:
                    xn_block(j // 4)

            nev = [0]

            def advance(upto):
                while KF1 and nev[0] <= upto:
                    evac(nev[0])
                    nev[0] += 1

            for i in range(NT + 1):
                if i < NT:
                    sc_inv(s, i)
                if i >= 1:
                    j = i - 1
                    sc_finish(s, j)
                    s.activation(
                        out=xw[j % NXW][:], in_=xw[j % NXW][:], func=AF.Identity,
                        scale=st_A[:, j : j + 1], bias=st_B[:, j : j + 1],
                    ).then_inc(s_t.h, 1)
                    s_t.inc()
                    s.wait_ge(s_t.h, s_t.n)
                    s.dma_start(out=tt[j], in_=xw[j % NXW][:]).then_inc(
                        s_st4[j % NXW].h, 16
                    )
                    s_st4[j % NXW].inc(16)
                    advance(j - 2)
            advance(NT - 1)

        def fA_te(t):
            t.wait_ge(s_const.h, CONSTS)
            if psum_pred:
                for sem, val in psum_pred:
                    t.wait_ge(sem.h, val)
            for i in range(NT):
                t.wait_ge(s_t.h, b["t"] + i + 1)
                if i >= 4:
                    t.wait_ge(s_tev.h, b["tev"] + i - 3)
                pb = (i % 4) * 2
                for c in range(NC8):
                    mm = t.matmul(
                        bank(pb + c // 4, 128 * (c % 4), 128 * (c % 4) + 128),
                        xw[i % NXW][:, 128 * c : 128 * (c + 1)],
                        identF[:],
                        start=(c % 4 == 0), stop=(c % 4 == 3), is_transpose=True,
                    )
                    if c == NC8 - 1:
                        mm.then_inc(s_tp.h, 1)
                s_tp.inc()

        sec(blk.scalar, fA_sc)
        if KF1:
            sec(blk.tensor, fA_te)

        s_ccw = S(f"sccw_{tag}")
        s_ccd = S(f"sccd_{tag}")
        s_ccr = S(f"sccr_{tag}")
        s_rw = S(f"srw_{tag}")
        s_bc = S(f"sbc_{tag}")

        def fC_gp_a(g):
            g.wait_ge(s_mx.h, b["mx"] + 1)
            g.dma_start(out=cc_in[cc_idx][0:1, :], in_=smax[:, 0:1]).then_inc(
                s_ccw.h, 16
            )
            s_ccw.inc(16)
            g.wait_ge(s_ccw.h, s_ccw.n)
            g.collective_compute(
                "AllReduce", OP.max,
                replica_groups=[list(range(n_cores))],
                ins=[cc_in[cc_idx].opt()],
                outs=[cc_out[cc_idx].opt()],
            ).then_inc(s_ccd.h, 1)
            s_ccd.inc()
            g.wait_ge(s_ccd.h, s_ccd.n)
            g.dma_start(out=grow[0:1, :], in_=cc_out[cc_idx][0:1, :]).then_inc(
                s_ccr.h, 16
            )
            s_ccr.inc(16)

        if KF1:
            sec(blk.gpsimd, fC_gp_a)
        ccr_val = s_ccr.n

        s_qs = S(f"sqs_{tag}")
        qs_base = s_qs.n

        def fC_vec1(v):
            v.wait_ge(s_ccr.h, ccr_val)
            if compute_alpha:
                v.wait_ge(s_alf.h, 16)
                ev = strided1(arow.ap()[0:1, 0:1], 2, 128)
                od = strided1(arow.ap()[0:1, 1:2], 2, 128)
                v.reduce_sum(out=s11[0:1, 4:5], in_=ev, axis=AX)
                v.drain()
                v.reduce_sum(out=s11[0:1, 5:6], in_=od, axis=AX)
                v.drain()
                v.tensor_scalar(out=s11[0:1, 6:7], in0=s11[0:1, 5:6], scalar1=1.0,
                                scalar2=None, op0=OP.max)
                v.drain()
                v.reciprocal(out=s11[0:1, 6:7], in_=s11[0:1, 6:7])
                v.drain()
                v.tensor_tensor(out=s11[0:1, 6:7], in0=s11[0:1, 6:7],
                                in1=s11[0:1, 4:5], op=OP.mult)
                v.drain()
            v.reduce_max(out=s11[0:1, 0:1], in_=grow[0:1, :], axis=AX)
            v.drain()
            v.tensor_scalar(
                out=s11[0:1, 1:2], in0=s11[0:1, 0:1],
                scalar1=float(np.float32(1e-8)), scalar2=1.0 / 127.0,
                op0=OP.max, op1=OP.mult,
            )
            v.drain()
            v.reciprocal(out=s11[0:1, 2:3], in_=s11[0:1, 1:2])
            v.drain()
            v.tensor_tensor(out=s11[0:1, 3:4], in0=s11[0:1, 1:2],
                            in1=s11[0:1, 6:7], op=OP.mult)
            v.drain()
            v.tensor_scalar(out=s11[0:1, 3:4], in0=s11[0:1, 3:4], scalar1=0.5,
                            scalar2=None, op0=OP.mult)
            v.drain()
            v.tensor_copy(brow[0:1, 0:128], rep0(s11.ap()[0:1, 2:3], 128))
            v.drain()
            v.tensor_copy(brow[0:1, 128:256], rep0(s11.ap()[0:1, 3:4], 128))
            v.drain().then_inc(s_rw.h, 1)
            s_rw.inc()

        if KF1:
            sec(blk.vector, fC_vec1)

        def fC_gp_b(g):
            g.wait_ge(s_rw.h, s_rw.n)
            g.dma_start(out=invscol[:, 0:1], in_=brow[0:1, 0:128]).then_inc(
                s_bc.h, 16
            )
            s_bc.inc(16)
            g.dma_start(out=hafcol[:, 0:1], in_=brow[0:1, 128:256]).then_inc(
                s_bc.h, 16
            )
            s_bc.inc(16)

        if KF1:
            sec(blk.gpsimd, fC_gp_b)
        bc_val = s_bc.n

        def fC_vec2(v):
            v.wait_ge(s_bc.h, bc_val)
            v.tensor_scalar(out=st_cA[:], in0=st_A[:], scalar1=hafcol[:, 0:1],
                            scalar2=None, op0=OP.mult)
            v.drain()
            v.tensor_scalar(out=gscols[:], in0=gcols[:], scalar1=invscol[:, 0:1],
                            scalar2=None, op0=OP.mult)
            v.drain()
            v.tensor_scalar(
                out=bsccols[:], in0=bcols[:], scalar1=invscol[:, 0:1],
                scalar2=None, op0=OP.mult,
            )
            v.drain().then_inc(s_qs.h, 1)
            s_qs.inc()

        if KF1:
            sec(blk.vector, fC_vec2)

        s_txf = S(f"stxf_{tag}")
        s_tp2 = S(f"stp2_{tag}")
        s_tev2 = S(f"stev2_{tag}")
        s_q8 = S(f"sq8_{tag}")
        s_nb = S(f"snb_{tag}")
        s_fm = S(f"sfm_{tag}")
        s_e1 = S(f"se1_{tag}")
        s_o = S(f"so_{tag}")
        b2 = dict(t=s_t.n, txf=s_txf.n, tp2=s_tp2.n, tev2=s_tev2.n, q8=s_q8.n,
                  nb=s_nb.n, fm=s_fm.n, e1=s_e1.n, o=s_o.n, qs=qs_base,
                  ld=[s_ld[k].n for k in range(NXW)])
        bstB = [s_st4[k].n for k in range(NXW)]

        def fB_sync(sy):
            sy.wait_ge(s_tp.h, s_tp.n)
            for k in range(NXW):
                sy.wait_ge(s_st4[k].h, bstA[k] + 16 * (NT // NXW))
            for i in range(NT):
                if i >= NXW:
                    sy.wait_ge(s_st4[i % NXW].h, bstB[i % NXW] + 16 * (i // NXW))
                    if post_slot_gate is not None:
                        post_slot_gate(sy, i)
                sy.dma_start(out=xw[i % NXW][:], in_=tt[i]).then_inc(
                    s_ld[i % NXW].h, 16
                )
                s_ld[i % NXW].inc(16)

        def fB_te(t):
            def transposes(g):
                for j in range(4):
                    i = 4 * g + j
                    t.wait_ge(s_ld[i % NXW].h, b2["ld"][i % NXW] + 16 * (i // NXW + 1))
                    if g == 0 and j == 0:
                        t.wait_ge(s_tev.h, b["tev"] + NT)
                    if g >= 1 and j == 0:
                        t.wait_ge(s_nb.h, b2["nb"] + g)
                    if i >= 2:
                        t.wait_ge(s_tev2.h, b2["tev2"] + i - 1)
                    pb = (i % 2) * 2
                    for c in range(NC8):
                        mm = t.matmul(
                            bank(pb + c // 4, 128 * (c % 4), 128 * (c % 4) + 128),
                            xw[i % NXW][:, 128 * c : 128 * (c + 1)],
                            identF[:],
                            start=(c % 4 == 0), stop=(c % 4 == 3), is_transpose=True,
                        )
                        if c == NC8 - 1:
                            mm.then_inc(s_tp2.h, 1)
                    s_tp2.inc()

            def ffns(g):
                for j in range(4):
                    i = 4 * g + j
                    t.wait_ge(s_nb.h, b2["nb"] + g + 1)
                    if i == 0:
                        t.wait_ge(s_wsg.h, NC8)
                    if i >= 2:
                        t.wait_ge(s_e1.h, b2["e1"] + i - 1)
                    for oh in range(2):
                        for c in range(NC8):
                            mm = t.matmul(
                                bank(4 + (i % 2) * 2 + oh),
                                nTg[:, 512 * c + 128 * j : 512 * c + 128 * (j + 1)],
                                wsgn[:, c, 512 * oh : 512 * (oh + 1)],
                                start=(c == 0), stop=(c == NC8 - 1),
                            )
                            if oh == 1 and c == NC8 - 1:
                                mm.then_inc(s_fm.h, 1)
                    s_fm.inc()

            transposes(0)
            for g in range(1, NG):
                ffns(g - 1)
                transposes(g)
            ffns(NG - 1)

        def fB_vec(v):
            for g in range(NG):
                v.wait_ge(s_q8.h, b2["q8"] + g + 1)
                v.tensor_scalar(
                    out=nTg[:], in0=tTg[:], scalar1=C_ROUND, scalar2=C_ROUND,
                    op0=OP.add, op1=OP.subtract,
                )
                v.drain().then_inc(s_nb.h, 1)
                s_nb.inc()
                for j in range(4):
                    i = 4 * g + j
                    v.wait_ge(s_fm.h, b2["fm"] + i + 1)
                    v.scalar_tensor_tensor(
                        out=xw[i % NXW][:], in0=bankpair(4 + (i % 2) * 2),
                        scalar=st_cA[:, i : i + 1], in1=xw[i % NXW][:],
                        op0=OP.mult, op1=OP.add,
                    )
                    v.drain().then_inc(s_e1.h, 1)
                    s_e1.inc()
                    if post_vec is not None:
                        post_vec(v, i, b2["t"] + i + 1)
            if post_vec is not None:
                post_vec(v, NT, 0)

        def fB_sc(s):
            s.wait_ge(s_qs.h, b2["qs"] + 1)
            for g in range(NG):
                for j in range(4):
                    i = 4 * g + j
                    s.wait_ge(s_tp2.h, b2["tp2"] + i + 1)
                    s.activation(
                        out=tTg_cw[:, :, 128 * j : 128 * (j + 1)],
                        in_=bankpair((i % 2) * 2), func=AF.Identity,
                    ).then_inc(s_tev2.h, 1)
                    s_tev2.inc()
                for c in range(NC8):
                    act = s.activation(
                        out=tTg[:, 512 * c : 512 * (c + 1)],
                        in_=tTg[:, 512 * c : 512 * (c + 1)],
                        func=AF.Identity,
                        scale=gscols[:, c : c + 1], bias=bsccols[:, c : c + 1],
                    )
                    if c == NC8 - 1:
                        act.then_inc(s_q8.h, 1)
                s_q8.inc()
                for j in range(4):
                    i = 4 * g + j
                    s.wait_ge(s_e1.h, b2["e1"] + i + 1)
                    s.activation(
                        out=xw[i % NXW][:], in_=xw[i % NXW][:], func=AF.Identity,
                        scale=st_invA[:, i : i + 1],
                        bias=st_muv[:, 2 * i : 2 * i + 1],
                    ).then_inc(s_t.h, 1)
                    s_t.inc()
                    s.wait_ge(s_t.h, s_t.n)
                    s.dma_start(out=dst_tiles[i], in_=xw[i % NXW][:]).then_inc(
                        s_st4[i % NXW].h, 16
                    )
                    s_st4[i % NXW].inc(16)
                    if post_sc_group is not None:
                        post_sc_group(s, i)
            if post_sc_group is not None:
                post_sc_group(s, NT)

        if KF2:
            sec(blk.sync, fB_sync)
            sec(blk.tensor, fB_te)
            sec(blk.vector, fB_vec)
            sec(blk.scalar, fB_sc)
        return bstB

    s_bn2 = S("s_bn2")
    s_iv2 = S("s_iv2")
    s_sA2 = S("s_sA2")

    def post_vec1(v, i, tval):
        if i < NT:
            v.wait_ge(S("st_fS").h, tval)
            v.bn_stats(out=bn12[:, 0:6], in_=xw[i % NXW][:, 0:512])
            v.bn_stats(out=bn12[:, 6:12], in_=xw[i % NXW][:, 512:1024])
            v.drain()
            v.bn_aggr(out=s2_muv[:, 2 * i : 2 * i + 2], in_=bn12[:])
            v.drain().then_inc(s_bn2.h, 1)
            s_bn2.inc()
        if i >= 1:
            j = i - 1
            v.wait_ge(s_iv2.h, j + 1)
            v.reciprocal(out=s2_A[:, j : j + 1],
                         in_=stw[3][:, j % 4 : j % 4 + 1])
            v.drain().then_inc(s_sA2.h, 1)
            s_sA2.inc()

    def post_sc1(s, i):
        if i < NT:
            s.wait_ge(s_bn2.h, i + 1)
            vv = s2_muv.ap()[:, 2 * i + 1 : 2 * i + 2]
            s.activation(out=stw[3].ap()[:, i % 4 : i % 4 + 1], in_=vv,
                         func=AF.Sqrt, bias=ccst[:, 0:1]).then_inc(s_iv2.h, 1)
            s_iv2.inc()
        if i >= 1:
            j = i - 1
            s.wait_ge(s_sA2.h, j + 1)
            w2 = stw[2].ap()[:, j % 4 : j % 4 + 1]
            mu = s2_muv.ap()[:, 2 * j : 2 * j + 1]
            s.activation(out=w2, in_=mu, func=AF.Identity, scale=ccst[:, 3:4])
            s.activation(out=s2_B[:, j : j + 1], in_=w2, func=AF.Identity,
                         scale=s2_A.ap()[:, j : j + 1])

    def post_slot1(sy, i):
        sy.wait_ge(s_bn2.h, i - NXW + 1)

    bstX1 = emit_ffn_stage(1, [xt[i] for i in range(NT)],
                           [x1t[i] for i in range(NT)], 0, None, True,
                           post_vec=post_vec1, post_slot_gate=post_slot1,
                           post_sc_group=post_sc1)

    s_wqk = S("s_wqk")

    def wqkv_gp(g):
        g.wait_ge(s_as2.h, NC8)
        for c in range(NC8):
            g.dma_start(
                out=wqkv[:, c, :], in_=wqkvT_h[128 * c : 128 * (c + 1), :]
            ).then_inc(s_wqk.h, 16)
            s_wqk.inc(16)

    sec(blk.gpsimd, wqkv_gp)

    if stages < 2:
        def fin_sync1(sy):
            for k in range(NXW):
                sy.wait_ge(s_st4[k].h, s_st4[k].n)
        sec(blk.sync, fin_sync1)
        stack.close()
        return nc

    s_h2 = S("sh2_a")
    s_h2p = S("sh2p_a")
    s_h2e = S("sh2e_a")
    s_kp = S("skp_a")
    s_kmin = S("skmin_a")
    s_kexp = S("skexp_a")
    s_kev = S("skev_a")
    s_ka = S("ska_a")
    s_kadd = S("skadd_a")
    s_kvb = S("skvb_a")
    aB = dict(h2=s_h2.n, h2p=s_h2p.n, h2e=s_h2e.n, kp=s_kp.n, kmin=s_kmin.n,
              kexp=s_kexp.n, kev=s_kev.n, ka=s_ka.n, kadd=s_kadd.n,
              ld=[s_ld[k].n for k in range(NXW)])
    PRED_2A = [(S("stev2_fS"), S("stev2_fS").n), (S("se1_fS"), S("se1_fS").n)]

    def s2a_sync(sy):
        for i in range(NT):
            if i < NXW:
                sy.wait_ge(s_bn2.h, NT)
                sy.wait_ge(s_st4[i % NXW].h, bstX1[i % NXW] + 16 * (NT // NXW))
            else:
                sy.wait_ge(s_h2.h, aB["h2"] + i - NXW + 1)
                sy.wait_ge(s_st4[i % NXW].h, bstX1[i % NXW] + 16 * (i // NXW + 1))
            sy.dma_start(out=xw[i % NXW][:], in_=x1t[i]).then_inc(
                s_ld[i % NXW].h, 16
            )
            s_ld[i % NXW].inc(16)

    sec(blk.sync, s2a_sync)

    def s2a_sc(s):
        s.wait_ge(s_sA2.h, NT)

        def h2_tile(i):
            s.wait_ge(s_ld[i % NXW].h, aB["ld"][i % NXW] + 16 * (i // NXW + 1))
            if i >= 2:
                s.wait_ge(s_h2p.h, aB["h2p"] + i - 1)
            s.activation(
                out=h2t[i % 2][:], in_=xw[i % NXW][:], func=AF.Identity,
                scale=s2_A[:, i : i + 1], bias=s2_B[:, i : i + 1],
            ).then_inc(s_h2.h, 1)
            s_h2.inc()

        def exp_tile(i):
            for og in range(2):
                s.wait_ge(s_kmin.h, aB["kmin"] + 2 * i + og + 1)
                s.activation(
                    out=scrB[:, 512 * og : 512 * (og + 1)],
                    in_=scrA[:, 512 * og : 512 * (og + 1)],
                    func=AF.Exp,
                ).then_inc(s_kexp.h, 1)
                s_kexp.inc()

        for i in range(min(4, NT)):
            h2_tile(i)
        for i in range(4, NT):
            h2_tile(i)
            exp_tile(i - 4)
        for i in range(max(0, NT - 4), NT):
            exp_tile(i)

    sec(blk.scalar, s2a_sc)

    def s2a_te(t):
        def transp_tile(i):
            t.wait_ge(s_h2.h, aB["h2"] + i + 1)
            if i == 0:
                t.wait_ge(s_wqk.h, 16 * NC8)
                for sem, val in PRED_2A:
                    t.wait_ge(sem.h, val)
            if i >= 4:
                t.wait_ge(s_h2e.h, aB["h2e"] + i - 3)
            for c in range(NC8):
                mm = t.matmul(
                    bankb16(i % 4)[:, 128 * c : 128 * (c + 1)],
                    h2t[i % 2][:, 128 * c : 128 * (c + 1)],
                    identB[:],
                    start=(c == 0), stop=(c == NC8 - 1), is_transpose=True,
                )
                if c == NC8 - 1:
                    mm.then_inc(s_h2p.h, 1)
            s_h2p.inc()

        def kv_tile(i):
            t.wait_ge(s_h2e.h, aB["h2e"] + i + 1)
            for og in range(4):
                idx = 4 * i + og
                if idx >= 2:
                    t.wait_ge(s_kev.h, aB["kev"] + idx - 1)
                for c in range(NC8):
                    mm = t.matmul(
                        bank(4 + og % 2),
                        h2Tt[i % 4][:, 128 * c : 128 * (c + 1)],
                        wqkv[:, c, D + 512 * og : D + 512 * (og + 1)],
                        start=(c == 0), stop=(c == NC8 - 1),
                    )
                    if c == NC8 - 1:
                        mm.then_inc(s_kp.h, 1)
                s_kp.inc()
            t.wait_ge(s_kev.h, aB["kev"] + 4 * (i + 1))
            for q in range(2):
                if 2 * i + q >= 1:
                    t.wait_ge(s_kadd.h, aB["kadd"] + 2 * i + q)
                for c in range(4 * q, 4 * q + 4):
                    cc = c - 4 * q
                    mm = t.matmul(
                        bank(6 + cc // 2, 130 * (cc % 2), 130 * (cc % 2) + 130),
                        ktile[i % 2][:, 128 * c : 128 * (c + 1)],
                        vsb[i % 2][:, 130 * c : 130 * (c + 1)],
                        start=(cc % 2 == 0), stop=(cc % 2 == 1),
                    )
                    if cc == 3:
                        mm.then_inc(s_ka.h, 1)
                s_ka.inc()

        for i in range(min(4, NT)):
            transp_tile(i)
        for i in range(NT - 4):
            kv_tile(i)
            transp_tile(i + 4)
        for i in range(max(0, NT - 4), NT):
            kv_tile(i)

    sec(blk.tensor, s2a_te)

    def s2a_vec(v):
        def evac_tile2(i):
            v.wait_ge(s_h2p.h, aB["h2p"] + i + 1)
            if i >= 4:
                v.wait_ge(s_kp.h, aB["kp"] + 4 * (i - 3))
            v.tensor_copy(h2Tt[i % 4][:], bankb16(i % 4))
            v.drain().then_inc(s_h2e.h, 1)
            s_h2e.inc()

        def kvelem_tile2(i):
            for og in range(4):
                idx = 4 * i + og
                v.wait_ge(s_kp.h, aB["kp"] + idx + 1)
                if i >= 2 and og == 0:
                    v.wait_ge(s_ka.h, aB["ka"] + 2 * (i - 1))
                pk = bank(4 + og % 2)
                if og < 2:
                    v.tensor_scalar(
                        out=scrA[:, 512 * og : 512 * (og + 1)], in0=pk,
                        scalar1=0.0, scalar2=None, op0=OP.min,
                    )
                    v.drain().then_inc(s_kmin.h, 1)
                    s_kmin.inc()
                    v.wait_ge(s_kexp.h, aB["kexp"] + 2 * i + og + 1)
                    v.scalar_tensor_tensor(
                        out=ktile[i % 2][:, 512 * og : 512 * (og + 1)], in0=pk,
                        scalar=0.0, in1=scrB[:, 512 * og : 512 * (og + 1)],
                        op0=OP.max, op1=OP.add,
                    )
                    v.drain().then_inc(s_kev.h, 1)
                    s_kev.inc()
                else:
                    vo = og - 2
                    dst = vsb[i % 2].ap().rearrange("p (h c) -> p h c", c=65)[
                        :, 8 * vo : 8 * (vo + 1), 0:64
                    ]
                    v.tensor_copy(dst, pk)
                    v.drain().then_inc(s_kev.h, 1)
                    s_kev.inc()
            for q in range(2):
                v.wait_ge(s_ka.h, aB["ka"] + 2 * i + q + 1)
                psrc = PS.ap()[:, 512 * 6 : 512 * 8].rearrange(
                    "p (b s) -> p b s", b=2)[:, :, 0:260]
                kdst = kvf.ap()[:, 520 * q : 520 * (q + 1)].rearrange(
                    "p (b c) -> p b c", b=2)
                v.tensor_tensor(out=kdst, in0=kdst, in1=psrc, op=OP.add)
                v.drain().then_inc(s_kadd.h, 1)
                s_kadd.inc()

        for i in range(min(4, NT)):
            evac_tile2(i)
        for i in range(4, NT):
            kvelem_tile2(i - 4)
            evac_tile2(i)
        for i in range(max(0, NT - 4), NT):
            kvelem_tile2(i)
        kv3 = kvf.ap().rearrange("p (c w) -> p c w", w=130)
        v.memset(kv3[64:128, :, 0:65], 0.0)
        v.drain()
        v.memset(kv3[0:64, :, 65:130], 0.0)
        v.drain()
        v.tensor_copy(kvb[:], kvf[:])
        v.drain().then_inc(s_kvb.h, 1)
        s_kvb.inc()

    sec(blk.vector, s2a_vec)

    s_wpk = S("s_wpk")

    def wproj_gp(g):
        for c in range(NC8):
            g.dma_start(
                out=wproj[:, c, :], in_=wprojT_h[128 * c : 128 * (c + 1), :]
            ).then_inc(s_wpk.h, 16)
            s_wpk.inc(16)

    sec(blk.gpsimd, wproj_gp)

    s_xf2b = S("sxf_s2b")
    s_h2b = S("sh2_b")
    s_h2pb = S("sh2p_b")
    s_h2eb = S("sh2e_b")
    s_qm = S("sqm_b")
    s_qmin = S("sqmin_b")
    s_qexp = S("sqexp_b")
    s_qe = S("sqe_b")
    s_nd = S("snd_b")
    s_dv = S("sdv_b")
    s_ap = S("sap_b")
    s_ae = S("sae_b")
    s_pj = S("spj_b")
    s_rs = S("srs_b")
    s_o2 = S("so_s2b")
    bB = dict(xf=s_xf2b.n, h2=s_h2b.n, h2p=s_h2pb.n, h2e=s_h2eb.n,
              qm=s_qm.n, qmin=s_qmin.n, qexp=s_qexp.n, qe=s_qe.n, nd=s_nd.n,
              dv=s_dv.n, ap=s_ap.n, ae=s_ae.n, pj=s_pj.n, rs=s_rs.n, o=s_o2.n,
              ld=[s_ld[k].n for k in range(NXW)])
    bstX2 = [s_st4[k].n for k in range(NXW)]

    def s2b_sync(sy):
        sy.wait_ge(s_h2.h, s_h2.n)
        for i in range(NT):
            if i >= NXW:
                sy.wait_ge(s_st4[i % NXW].h, bstX2[i % NXW] + 16 * (i // NXW))
            sy.dma_start(out=xw[i % NXW][:], in_=x1t[i]).then_inc(
                s_ld[i % NXW].h, 16
            )
            s_ld[i % NXW].inc(16)

    def s2b_sc(s):
        for g in range(NG):
            for j in range(4):
                i = 4 * g + j
                s.wait_ge(s_ld[i % NXW].h, bB["ld"][i % NXW] + 16 * (i // NXW + 1))
                if i >= 2:
                    s.wait_ge(s_h2pb.h, bB["h2p"] + i - 1)
                s.activation(
                    out=h2t[i % 2][:], in_=xw[i % NXW][:], func=AF.Identity,
                    scale=s2_A[:, i : i + 1], bias=s2_B[:, i : i + 1],
                ).then_inc(s_h2b.h, 1)
                s_h2b.inc()
            for oc in range(NC8):
                idx = NC8 * g + oc
                s.wait_ge(s_qmin.h, bB["qmin"] + idx + 1)
                s.activation(out=scrB[:, 0:512], in_=scrA[:, 0:512],
                             func=AF.Exp).then_inc(s_qexp.h, 1)
                s_qexp.inc()
            for j in range(4):
                i = 4 * g + j
                s.wait_ge(s_rs.h, bB["rs"] + i + 1)
                s.dma_start(out=x2t[i], in_=xw[i % NXW][:]).then_inc(
                    s_st4[i % NXW].h, 16
                )
                s_st4[i % NXW].inc(16)

    PRED_2B = [(s_h2e, s_h2e.n), (s_kev, s_kev.n), (s_kadd, s_kadd.n)]

    def s2b_te(t):
        for g in range(NG):
            for j in range(4):
                i = 4 * g + j
                t.wait_ge(s_h2b.h, bB["h2"] + i + 1)
                if g == 0 and j == 0:
                    for sem, val in PRED_2B:
                        t.wait_ge(sem.h, val)
                if g >= 1 and j == 0:
                    t.wait_ge(s_pj.h, bB["pj"] + 8 * g)
                if i >= 2:
                    t.wait_ge(s_h2eb.h, bB["h2e"] + i - 1)
                for c in range(NC8):
                    mm = t.matmul(
                        bankb16(i % 2)[:, 128 * c : 128 * (c + 1)],
                        h2t[i % 2][:, 128 * c : 128 * (c + 1)],
                        identB[:],
                        start=(c == 0), stop=(c == NC8 - 1), is_transpose=True,
                    )
                    if c == NC8 - 1:
                        mm.then_inc(s_h2pb.h, 1)
                s_h2pb.inc()
            for oc in range(NC8):
                idx = NC8 * g + oc
                t.wait_ge(s_h2eb.h, bB["h2e"] + 4 * (g + 1))
                if idx == 0:
                    t.wait_ge(s_kvb.h, 1)
                if idx >= 2:
                    t.wait_ge(s_qe.h, bB["qe"] + idx - 1)
                for c in range(NC8):
                    mm = t.matmul(
                        bank(2 + oc % 2),
                        wqkv[:, c, 128 * oc : 128 * (oc + 1)],
                        h2Tg[:, 512 * c : 512 * (c + 1)],
                        start=(c == 0), stop=(c == NC8 - 1),
                    )
                    if c == NC8 - 1:
                        mm.then_inc(s_qm.h, 1)
                s_qm.inc()
                t.wait_ge(s_qe.h, bB["qe"] + idx + 1)
                if g >= 1 and oc == 0:
                    t.wait_ge(s_ae.h, bB["ae"] + 4 * g)
                    t.wait_ge(s_rs.h, bB["rs"] + 4 * g)
                for ts in range(4):
                    nd_idx = idx * 4 + ts
                    if nd_idx >= 4:
                        t.wait_ge(s_dv.h, bB["dv"] + nd_idx - 3)
                    t.matmul(
                        bank(4 + ts, 0, 130),
                        qg[:, 512 * oc + 128 * ts : 512 * oc + 128 * (ts + 1)],
                        kvb[:, 130 * oc : 130 * (oc + 1)],
                        start=True, stop=True,
                    ).then_inc(s_nd.h, 1)
                    s_nd.inc()
            for j in range(4):
                i = 4 * g + j
                t.wait_ge(s_dv.h, bB["dv"] + 32 * (g + 1))
                if i >= 1:
                    t.wait_ge(s_ae.h, bB["ae"] + i)
                for c in range(NC8):
                    mm = t.matmul(
                        bankb16(4)[:, 128 * c : 128 * (c + 1)],
                        attng_ap[:, 1024 * j + 128 * c : 1024 * j + 128 * (c + 1)],
                        identB[:],
                        start=(c == 0), stop=(c == NC8 - 1), is_transpose=True,
                    )
                    if c == NC8 - 1:
                        mm.then_inc(s_ap.h, 1)
                s_ap.inc()
                t.wait_ge(s_ae.h, bB["ae"] + i + 1)
                if i == 0:
                    t.wait_ge(s_wpk.h, 16 * NC8)
                    t.wait_ge(s_bpj.h, 16)
                for oh in range(2):
                    pidx = 2 * i + oh
                    if pidx >= 2:
                        t.wait_ge(s_rs.h, bB["rs"] + pidx // 2)
                    for c in range(NC8):
                        t.matmul(
                            bank(5 + oh),
                            h2Tg[:, 512 * c + 128 * j : 512 * c + 128 * (j + 1)],
                            wproj[:, c, 512 * oh : 512 * (oh + 1)],
                            start=(c == 0), stop=False,
                        )
                    t.matmul(
                        bank(5 + oh),
                        ones1[:, :],
                        bprojb[0:1, 512 * oh : 512 * (oh + 1)],
                        start=False, stop=True,
                    ).then_inc(s_pj.h, 1)
                    s_pj.inc()

    def s2b_vec(v):
        _a = PS.ap()[:, 512 * 4 + 64 : 512 * 4 + 65]
        nd_dencols = bass.AP(tensor=_a.tensor, offset=_a.offset,
                             ap=[list(_a.ap[0]), [512, 4], [65, 2]])
        rcb8 = rcb.ap()[:, 0:8].rearrange("p (b h) -> p b h", b=4)
        for g in range(NG):
            for j in range(4):
                i = 4 * g + j
                v.wait_ge(s_h2pb.h, bB["h2p"] + i + 1)
                v.tensor_copy(
                    h2Tg_cw[:, :, 128 * j : 128 * (j + 1)], bankb16(i % 2)
                )
                v.drain().then_inc(s_h2eb.h, 1)
                s_h2eb.inc()
            for oc in range(NC8):
                idx = NC8 * g + oc
                v.wait_ge(s_qm.h, bB["qm"] + idx + 1)
                pq = bank(2 + oc % 2)
                v.tensor_scalar(out=scrA[:, 0:512], in0=pq, scalar1=0.0,
                                scalar2=None, op0=OP.min)
                v.drain().then_inc(s_qmin.h, 1)
                s_qmin.inc()
                v.wait_ge(s_qexp.h, bB["qexp"] + idx + 1)
                v.scalar_tensor_tensor(
                    out=qg[:, 512 * oc : 512 * (oc + 1)], in0=pq, scalar=0.0,
                    in1=scrB[:, 0:512], op0=OP.max, op1=OP.add,
                )
                v.drain().then_inc(s_qe.h, 1)
                s_qe.inc()
                v.wait_ge(s_nd.h, bB["nd"] + 4 * idx + 4)
                v.tensor_scalar(out=rcb8, in0=nd_dencols,
                                scalar1=float(np.float32(1e-8)),
                                scalar2=None, op0=OP.add)
                v.drain()
                v.reciprocal(out=rrb[:, 0:8], in_=rcb[:, 0:8])
                v.drain()
                for ts in range(4):
                    ndap = PS.ap()[:, 512 * (4 + ts) : 512 * (4 + ts) + 130]
                    for hh in range(2):
                        v.tensor_scalar(
                            out=attng_ap[:, 1024 * ts + 128 * oc + 64 * hh :
                                         1024 * ts + 128 * oc + 64 * (hh + 1)],
                            in0=ndap[:, 65 * hh : 65 * hh + 64],
                            scalar1=rrb[:, 2 * ts + hh : 2 * ts + hh + 1],
                            scalar2=None, op0=OP.mult,
                        )
                v.drain().then_inc(s_dv.h, 4)
                s_dv.inc(4)
            for j in range(4):
                i = 4 * g + j
                v.wait_ge(s_ap.h, bB["ap"] + i + 1)
                v.tensor_copy(
                    h2Tg_cw[:, :, 128 * j : 128 * (j + 1)], bankb16(4)
                )
                v.drain().then_inc(s_ae.h, 1)
                s_ae.inc()
                v.wait_ge(s_pj.h, bB["pj"] + 2 * i + 2)
                v.scalar_tensor_tensor(
                    out=xw[i % NXW][:], in0=PS.ap()[:, 512 * 5 : 512 * 7],
                    scalar=0.0, in1=xw[i % NXW][:],
                    op0=OP.bypass, op1=OP.add,
                )
                v.drain().then_inc(s_rs.h, 1)
                s_rs.inc()

    sec(blk.sync, s2b_sync)
    sec(blk.scalar, s2b_sc)
    sec(blk.tensor, s2b_te)
    sec(blk.vector, s2b_vec)

    if stages < 3:
        def fin_sync2(sy):
            for k in range(NXW):
                sy.wait_ge(s_st4[k].h, s_st4[k].n)
        sec(blk.sync, fin_sync2)
        stack.close()
        return nc

    def gate_x2(sy, i):
        if i < NXW:
            sy.wait_ge(s_st4[i % NXW].h, bstX2[i % NXW] + 16 * (NT // NXW))
        else:
            sy.wait_ge(s_st4[i % NXW].h, bstX2[i % NXW] + 16 * (i // NXW + 1))

    emit_ffn_stage(3, [x2t[i] for i in range(NT)],
                   [outt[i] for i in range(NT)], 1, gate_x2, False,
                   psum_pred=[(s_rs, s_rs.n), (s_dv, s_dv.n), (s_ae, s_ae.n),
                              (s_qe, s_qe.n), (s_h2eb, s_h2eb.n)])

    def fin_sync(sy):
        for k in range(NXW):
            sy.wait_ge(s_st4[k].h, s_st4[k].n)

    sec(blk.sync, fin_sync)

    stack.close()
    return nc



def _prep_inputs(L0_attn_x, L0_attn_w_qkv, L0_attn_w_proj, L0_attn_b_proj,
                 L0_ffn_weight, L0_ffn_gamma, L0_ffn_beta):
    wqkvT = np.ascontiguousarray(np.asarray(L0_attn_w_qkv).T).astype(np.float32)
    wprojT = np.ascontiguousarray(np.asarray(L0_attn_w_proj).T).astype(np.float32)
    wffnT = np.ascontiguousarray(np.asarray(L0_ffn_weight).T).astype(np.float32)
    gcols = np.ascontiguousarray(
        np.asarray(L0_ffn_gamma).astype(np.float32).reshape(NC8, 128).T)
    bcols = np.ascontiguousarray(
        np.asarray(L0_ffn_beta).astype(np.float32).reshape(NC8, 128).T)
    bprojr = np.ascontiguousarray(
        np.asarray(L0_attn_b_proj).astype(np.float32).reshape(1, D))
    shared = dict(
        wqkvT=wqkvT, wprojT=wprojT, wffnT=wffnT, gcolsi=gcols, bcolsi=bcols,
        bprojr=bprojr,
        identF=np.eye(128, dtype=np.float32),
        identB=np.eye(128, dtype=ml_dtypes.bfloat16),
        ones1=np.ones((1, 128), dtype=ml_dtypes.bfloat16),
    )
    xs = np.asarray(L0_attn_x)
    in_maps = []
    for bidx in range(xs.shape[0]):
        m = dict(shared)
        m["x_in"] = np.ascontiguousarray(xs[bidx].astype(np.float32))
        in_maps.append(m)
    return in_maps


def kernel(L0_attn_x, L0_attn_w_qkv, L0_attn_w_proj, L0_attn_b_proj,
           L0_ffn_weight, L0_ffn_gamma, L0_ffn_beta):
    n_cores = 8
    in_maps = _prep_inputs(L0_attn_x, L0_attn_w_qkv, L0_attn_w_proj,
                           L0_attn_b_proj, L0_ffn_weight, L0_ffn_gamma,
                           L0_ffn_beta)
    nc = build_nc(T=4096, n_cores=n_cores)
    res = run_bass_kernel_spmd(nc, in_maps, core_ids=list(range(n_cores)))
    out = np.stack([np.asarray(r["out"]) for r in res.results], axis=0)
    return out.astype(np.float32)


# version 15 (final)
# speedup vs baseline: 1.0566x; 1.0056x over previous
from contextlib import ExitStack

import numpy as np
import ml_dtypes

import concourse.bass as bass
import concourse.mybir as mybir
from concourse.bass_utils import run_bass_kernel_spmd

F32 = mybir.dt.float32
BF16 = mybir.dt.bfloat16
AX = mybir.AxisListType.X
OP = mybir.AluOpType
AF = mybir.ActivationFunctionType

D = 1024
NC8 = 8
HEADS = 16
C_ROUND = 12582912.0
EPS = float(np.float32(1e-5))
F = np.float32


class Ctr:
    def __init__(self, stack, nc, name):
        self.h = stack.enter_context(nc.semaphore(name))
        self.n = 0

    def inc(self, k=1):
        self.n += k
        return self.n


def build_nc(T=4096, n_cores=8, stages=3):
    NT = T // 128
    NG = NT // 4
    nc = bass.Bass("TRN2", target_bir_lowering=False, debug=False)

    x_in = nc.dram_tensor("x_in", [T, D], F32, kind="ExternalInput").ap()
    wqkvT_h = nc.dram_tensor("wqkvT", [D, 3 * D], F32, kind="ExternalInput").ap()
    wprojT_h = nc.dram_tensor("wprojT", [D, D], F32, kind="ExternalInput").ap()
    wffnT_h = nc.dram_tensor("wffnT", [D, D], F32, kind="ExternalInput").ap()
    gcols_h = nc.dram_tensor("gcolsi", [128, NC8], F32, kind="ExternalInput").ap()
    bcols_h = nc.dram_tensor("bcolsi", [128, NC8], F32, kind="ExternalInput").ap()
    bprojr_h = nc.dram_tensor("bprojr", [1, D], F32, kind="ExternalInput").ap()
    identF_h = nc.dram_tensor("identF", [128, 128], F32, kind="ExternalInput").ap()
    identB_h = nc.dram_tensor("identB", [128, 128], BF16, kind="ExternalInput").ap()
    ones1_h = nc.dram_tensor("ones1", [1, 128], BF16, kind="ExternalInput").ap()
    out_h = nc.dram_tensor("out", [T, D], F32, kind="ExternalOutput").ap()

    t_d = nc.dram_tensor("t_scr", [T, D], F32).ap()
    x1_d = nc.dram_tensor("x1_scr", [T, D], F32).ap()
    x2_d = nc.dram_tensor("x2_scr", [T, D], F32).ap()
    cc_in = [nc.dram_tensor(f"cc_in{i}", [1, 128], F32).ap() for i in range(2)]
    cc_out = [nc.dram_tensor(f"cc_out{i}", [1, 128], F32).ap() for i in range(2)]

    xt = x_in.rearrange("(n p) d -> n p d", p=128)
    tt = t_d.rearrange("(n p) d -> n p d", p=128)
    x1t = x1_d.rearrange("(n p) d -> n p d", p=128)
    x2t = x2_d.rearrange("(n p) d -> n p d", p=128)
    outt = out_h.rearrange("(n p) d -> n p d", p=128)

    stack = ExitStack()
    e = stack.enter_context

    NXW = 8
    xw = [e(nc.sbuf_tensor(f"xw{i}", [128, D], F32)) for i in range(NXW)]
    tTg = e(nc.sbuf_tensor("tTg", [128, 4 * D], F32))
    nTg = e(nc.sbuf_tensor("nTg", [128, 4 * D], BF16))
    wsgn = e(nc.sbuf_tensor("wsgn", [128, NC8, D], BF16))
    wqkv = e(nc.sbuf_tensor("wqkv", [128, NC8, 3 * D], BF16))
    wproj = e(nc.sbuf_tensor("wproj", [128, NC8, D], BF16))
    scrA = e(nc.sbuf_tensor("scrA", [128, D], F32))
    scrB = e(nc.sbuf_tensor("scrB", [128, D], F32))
    h2t = [e(nc.sbuf_tensor(f"h2t{i}", [128, D], BF16)) for i in range(2)]
    h2Tt = [e(nc.sbuf_tensor(f"h2Tt{i}", [128, D], BF16)) for i in range(4)]
    h2Tg = e(nc.sbuf_tensor("h2Tg", [128, 4 * D], BF16))

    ktile = [e(nc.sbuf_tensor(f"ktile{i}", [128, D], BF16)) for i in range(2)]
    vsb = [e(nc.sbuf_tensor(f"vsb{i}", [128, HEADS * 65], BF16)) for i in range(2)]
    kvf = e(nc.sbuf_tensor("kvf", [128, NC8 * 130], F32))
    kvb = e(nc.sbuf_tensor("kvb", [128, NC8 * 130], BF16))
    st_muv = e(nc.sbuf_tensor("st_muv", [128, 2 * NT], F32))
    st_A = e(nc.sbuf_tensor("st_A", [128, NT], F32))
    st_B = e(nc.sbuf_tensor("st_B", [128, NT], F32))
    st_invA = e(nc.sbuf_tensor("st_invA", [128, NT], F32))
    st_cA = e(nc.sbuf_tensor("st_cA", [128, NT], F32))
    s2_muv = e(nc.sbuf_tensor("s2_muv", [128, 2 * NT], F32))
    s2_A = e(nc.sbuf_tensor("s2_A", [128, NT], F32))
    s2_B = e(nc.sbuf_tensor("s2_B", [128, NT], F32))
    stw = [e(nc.sbuf_tensor(f"stw{i}", [128, 4], F32)) for i in range(5)]
    bn12 = e(nc.sbuf_tensor("bn12", [128, 12], F32))
    maxcol = e(nc.sbuf_tensor("maxcol", [128, max(NG, 1)], F32))
    smax = e(nc.sbuf_tensor("smax", [128, 1], F32))
    grow = e(nc.sbuf_tensor("grow", [1, 128], F32))
    arow = e(nc.sbuf_tensor("arow", [1, 256], F32))
    drow = e(nc.sbuf_tensor("drow", [1, 128], F32))
    brow = e(nc.sbuf_tensor("brow", [1, 256], F32))
    s11 = e(nc.sbuf_tensor("s11", [1, 8], F32))
    invscol = e(nc.sbuf_tensor("invscol", [128, 1], F32))
    hafcol = e(nc.sbuf_tensor("hafcol", [128, 1], F32))
    gscols = e(nc.sbuf_tensor("gscols", [128, NC8], F32))
    bsccols = e(nc.sbuf_tensor("bsccols", [128, NC8], F32))
    gcols = e(nc.sbuf_tensor("gcols", [128, NC8], F32))
    bcols = e(nc.sbuf_tensor("bcols", [128, NC8], F32))
    sumabs = e(nc.sbuf_tensor("sumabs", [128, NC8], F32))
    cntc = e(nc.sbuf_tensor("cntc", [128, NC8], F32))
    msumc = e(nc.sbuf_tensor("msumc", [128, NC8], F32))
    al2 = e(nc.sbuf_tensor("al2", [128, 2], F32))
    deltac = e(nc.sbuf_tensor("deltac", [128, 1], F32))
    identF = e(nc.sbuf_tensor("identFs", [128, 128], F32))
    identB = e(nc.sbuf_tensor("identBs", [128, 128], BF16))
    ones1 = e(nc.sbuf_tensor("ones1s", [1, 128], BF16))
    bprojb = e(nc.sbuf_tensor("bprojb", [1, D], BF16))
    rcb = e(nc.sbuf_tensor("rcb", [128, 8], F32))
    rrb = e(nc.sbuf_tensor("rrb", [128, 8], F32))
    ccst = e(nc.sbuf_tensor("ccst", [128, 4], F32))

    qg = nTg
    attng_ap = tTg.ap().bitcast(BF16)
    wstage_ap = wqkv.ap().rearrange("p a b -> p (a b)").bitcast(F32)
    scrC_ap = nTg.ap().bitcast(F32)

    PS = e(nc.psum_tensor("PS", [128, 4096], F32))

    def bank(b, lo=0, hi=512):
        return PS[:, 512 * b + lo : 512 * b + hi]

    def bankpair(b):
        return PS.ap()[:, 512 * b : 512 * b + 1024]

    def bankb16(b):
        return PS.ap()[:, 512 * b : 512 * (b + 1)].bitcast(BF16)

    def rep0(src11, n):
        a = src11
        return bass.AP(tensor=a.tensor, offset=a.offset,
                       ap=[list(a.ap[0]), [0, n]])

    def strided1(src, step, count):
        a = src
        return bass.AP(tensor=a.tensor, offset=a.offset,
                       ap=[list(a.ap[0]), [step, count]])

    sems = {}
    by_handle = {}
    blk = e(nc.Block())

    def S(name):
        if name not in sems:
            c = Ctr(stack, nc, name)
            sems[name] = c
            by_handle[id(c.h)] = c
        return sems[name]

    _CHAIN_OPS = {
        "tensor_scalar", "tensor_tensor", "scalar_tensor_tensor", "tensor_copy",
        "reciprocal", "reduce_max", "reduce_sum", "bn_stats", "bn_aggr",
        "memset", "tensor_scalar_add", "tensor_scalar_mul", "activation",
        "copy",
    }
    chain_state = {}

    class _Shim:
        __slots__ = ("host", "ins")

        def __init__(self, host, ins):
            self.host = host
            self.ins = ins

        def then_inc(self, semh, k):
            self.host._attach(self.ins, semh, k)
            return self.ins

    class ChainEng:
        def __init__(self, eng, key):
            self._e = eng
            self._key = key
            if key not in chain_state:
                chain_state[key] = {"last": None, "pending": None,
                                    "ctr": S(f"chain_{key}")}

        def _flush(self):
            st = chain_state[self._key]
            if st["pending"] is not None:
                ctr = st["ctr"]
                st["pending"].then_inc(ctr.h, 1)
                ctr.inc()
                st["last"] = (ctr.h, ctr.n)
                st["pending"] = None

        def _attach(self, ins, semh, k):
            st = chain_state[self._key]
            ctr = by_handle[id(semh)]
            ins.then_inc(semh, k)
            st["last"] = (semh, ctr.n + k)
            if st["pending"] is ins:
                st["pending"] = None

        def wait_ge(self, h, v):
            self._e.wait_ge(h, v)

        def drain(self):
            st = chain_state[self._key]
            if st["pending"] is not None:
                return _Shim(self, st["pending"])
            ins = self._e.drain()
            st["pending"] = ins
            return _Shim(self, ins)

        def __getattr__(self, name):
            raw = getattr(self._e, name)
            if name not in _CHAIN_OPS:
                return raw

            def emit(*a, **kw):
                self._flush()
                st = chain_state[self._key]
                if st["last"] is not None:
                    self._e.wait_ge(st["last"][0], st["last"][1])
                ins = raw(*a, **kw)
                st["pending"] = ins
                return _Shim(self, ins)

            return emit

    import os
    _seclim = int(os.environ.get("KSEC_LIMIT", "9999"))
    KF1 = os.environ.get("KF1", "1") == "1"
    KF2 = os.environ.get("KF2", "1") == "1"
    _secn = [0]

    def sec(dec, fn):
        _secn[0] += 1
        if _secn[0] <= _seclim:
            if getattr(dec, "__name__", "") == "scalar":
                dec(lambda e2: fn(ChainEng(e2, "s")))
            else:
                dec(fn)

    s_ld = [S(f"s_ld{k}") for k in range(NXW)]
    s_st4 = [S(f"s_st{k}") for k in range(NXW)]
    s_ws = [S(f"s_ws{k}") for k in range(2)]

    wst = wstage_ap[:, 0:2048].rearrange("p (s k) -> p s k", s=2)
    tTg_cw = tTg.ap().rearrange("p (c w) -> p c w", c=NC8)
    h2Tg_cw = h2Tg.ap().rearrange("p (c w) -> p c w", c=NC8)

    s_const = S("s_const")

    def ph0_sync(sy):
        for dst, src in [
            (gcols[:], gcols_h[:, :]), (bcols[:], bcols_h[:, :]),
            (identF[:], identF_h[:, :]), (identB[:], identB_h[:, :]),
            (ones1[:], ones1_h[:, :]),
        ]:
            sy.dma_start(out=dst, in_=src).then_inc(s_const.h, 16)
            s_const.inc(16)

    sec(blk.sync, ph0_sync)
    CONSTS = s_const.n

    s_zr = S("s_zr")

    def ph0_vec(v):
        v.memset(kvf[:], 0.0)
        v.memset(ccst[:, 0:1], EPS)
        v.memset(ccst[:, 1:2], float(np.float32(EPS) * np.float32(EPS)))
        v.memset(ccst[:, 2:3], float(np.float32(1.0) + np.float32(EPS)))
        v.memset(ccst[:, 3:4], -1.0)
        for i in range(2):
            v.memset(vsb[i].ap().rearrange("p (h c) -> p h c", c=65)[:, :, 64:65], 1.0)
        v.drain().then_inc(s_zr.h, 1)
        s_zr.inc()

    sec(blk.vector, ph0_vec)

    s_bpj = S("s_bpj")

    def ph0_gp(g):
        g.dma_start(out=bprojb[:], in_=bprojr_h[:, :]).then_inc(s_bpj.h, 16)
        s_bpj.inc(16)

    sec(blk.gpsimd, ph0_gp)

    s_wf1 = S("s_wf1")
    s_wf1u = S("s_wf1u")
    s_wf2 = S("s_wf2")
    s_wf2u = S("s_wf2u")
    s_abs1 = S("s_abs1")
    s_dl1 = S("s_dl1")
    s_dl2 = S("s_dl2")
    s_dl3 = S("s_dl3")
    s_dl4 = S("s_dl4")
    s_as2 = S("s_as2")
    s_msk = S("s_msk")
    s_wsg = S("s_wsg")
    s_al1 = S("s_al1")
    s_alf = S("s_alf")

    def wffn_gp(g):
        for c in range(NC8):
            if c >= 2:
                g.wait_ge(s_abs1.h, c - 1)
            g.dma_start(
                out=wst[:, c % 2, :], in_=wffnT_h[128 * c : 128 * (c + 1), :]
            ).then_inc(s_ws[c % 2].h, 16)
            s_ws[c % 2].inc(16)
        g.wait_ge(s_dl1.h, 1)
        g.dma_start(out=drow[0:1, :], in_=deltac[:, 0:1]).then_inc(s_dl2.h, 16)
        s_dl2.inc(16)
        g.wait_ge(s_dl3.h, 1)
        g.dma_start(out=deltac[:, 0:1], in_=rep0(drow.ap()[0:1, 0:1], 128)).then_inc(
            s_dl4.h, 16
        )
        s_dl4.inc(16)
        for c in range(NC8):
            if c == 0:
                g.wait_ge(s_abs1.h, NC8)
            if c >= 2:
                g.wait_ge(s_as2.h, c - 1)
            g.dma_start(
                out=wst[:, c % 2, :], in_=wffnT_h[128 * c : 128 * (c + 1), :]
            ).then_inc(s_ws[c % 2].h, 16)
            s_ws[c % 2].inc(16)
        g.wait_ge(s_al1.h, 1)
        g.dma_start(out=arow[0:1, :], in_=al2[:, :]).then_inc(s_alf.h, 16)
        s_alf.inc(16)

    sec(blk.gpsimd, wffn_gp)

    def wffn_sc(s):
        ws_base = [16 * 4 * 0, 0]
        for c in range(NC8):
            s.wait_ge(s_ws[c % 2].h, 16 * (c // 2 + 1))
            s.activation(
                out=scrA[:], in_=wst[:, c % 2, :], func=AF.Abs,
                accum_out=sumabs[:, c : c + 1],
            ).then_inc(s_abs1.h, 1)
            s_abs1.inc()
            s.drain()
        for c in range(NC8):
            s.wait_ge(s_ws[c % 2].h, 16 * 4 + 16 * (c // 2 + 1))
            if c >= 1:
                s.wait_ge(s_msk.h, c)
            s.activation(out=scrA[:], in_=wst[:, c % 2, :], func=AF.Abs)
            s.activation(out=scrB[:], in_=wst[:, c % 2, :], func=AF.Sign).then_inc(
                s_as2.h, 1
            )
            s_as2.inc()
            s.nop().then_inc(s_wf2u.h, 1)
            s_wf2u.inc()

    sec(blk.scalar, wffn_sc)

    def wffn_vec(v):
        v.wait_ge(s_abs1.h, NC8)
        v.reduce_sum(out=deltac[:], in_=sumabs[:], axis=AX)
        v.drain().then_inc(s_dl1.h, 1)
        s_dl1.inc()
        v.wait_ge(s_dl2.h, 16)
        v.reduce_sum(out=s11[0:1, 7:8], in_=drow[0:1, :], axis=AX)
        v.drain()
        v.tensor_scalar(
            out=drow[0:1, 0:1], in0=s11[0:1, 7:8],
            scalar1=float(np.float32(0.7)) / (D * D), scalar2=None, op0=OP.mult,
        )
        v.drain().then_inc(s_dl3.h, 1)
        s_dl3.inc()
        for c in range(NC8):
            v.wait_ge(s_as2.h, c + 1)
            if c == 0:
                v.wait_ge(s_dl4.h, 16)
            v.scalar_tensor_tensor(
                out=scrC_ap[:, 0:D], in0=scrA[:], scalar=deltac[:, 0:1], in1=scrA[:],
                op0=OP.is_gt, op1=OP.mult, accum_out=msumc[:, c : c + 1],
            )
            v.drain()
            v.tensor_scalar(
                out=scrC_ap[:, 0:D], in0=scrA[:], scalar1=deltac[:, 0:1], scalar2=0.0,
                op0=OP.is_gt, op1=OP.add, accum_out=cntc[:, c : c + 1],
            )
            v.drain()
            v.tensor_tensor(out=wsgn[:, c, :], in0=scrC_ap[:, 0:D], in1=scrB[:], op=OP.mult)
            v.drain().then_inc(s_wsg.h, 1)
            s_wsg.inc()
            v.drain().then_inc(s_msk.h, 1)
            s_msk.inc()
        v.reduce_sum(out=al2[:, 0:1], in_=msumc[:], axis=AX)
        v.drain()
        v.reduce_sum(out=al2[:, 1:2], in_=cntc[:], axis=AX)
        v.drain().then_inc(s_al1.h, 1)
        s_al1.inc()

    sec(blk.vector, wffn_vec)

    def emit_bn(tag, src_tiles, muv, sA, sB, sInvA, two_r, gate=None,
                slot_gate=None, vec_tile=None, vec_tail=None):
        s_bn = S(f"sbn_{tag}")
        bases = dict(bn=s_bn.n, ld=[s_ld[k].n for k in range(NXW)])

        def f_sync(sy):
            for i in range(NT):
                if gate is not None:
                    gate(sy, i)
                if i >= NXW and slot_gate is not None:
                    slot_gate(sy, i)
                sy.dma_start(out=xw[i % NXW][:], in_=src_tiles[i]).then_inc(
                    s_ld[i % NXW].h, 16
                )
                s_ld[i % NXW].inc(16)

        s_iv = S(f"siv_{tag}")
        bases["iv"] = s_iv.n

        def f_vec(v):
            for i in range(NT):
                v.wait_ge(s_ld[i % NXW].h, bases["ld"][i % NXW] + 16 * (i // NXW + 1))
                v.bn_stats(out=bn12[:, 0:6], in_=xw[i % NXW][:, 0:512])
                v.bn_stats(out=bn12[:, 6:12], in_=xw[i % NXW][:, 512:1024])
                v.drain()
                v.bn_aggr(out=muv[:, 2 * i : 2 * i + 2], in_=bn12[:])
                if i >= 2:
                    v.wait_ge(s_iv.h, bases["iv"] + i - 1)
                    v.reciprocal(out=sA[:, i - 2 : i - 1],
                                 in_=sInvA[:, i - 2 : i - 1])
                v.drain().then_inc(s_bn.h, 1)
                s_bn.inc()
                if vec_tile is not None:
                    vec_tile(v, i)
            v.wait_ge(s_iv.h, bases["iv"] + NT - 1)
            v.reciprocal(out=sA[:, NT - 2 : NT - 1], in_=sInvA[:, NT - 2 : NT - 1])
            v.drain().then_inc(s_bn.h, 1)
            s_bn.inc()
            v.wait_ge(s_iv.h, bases["iv"] + NT)
            v.reciprocal(out=sA[:, NT - 1 : NT], in_=sInvA[:, NT - 1 : NT])
            v.drain().then_inc(s_bn.h, 1)
            s_bn.inc()
            if vec_tail is not None:
                vec_tail(v)

        sec(blk.sync, f_sync)
        sec(blk.vector, f_vec)

        def sc_inv(s, i):
            s.wait_ge(s_bn.h, bases["bn"] + i + 1)
            vv = muv.ap()[:, 2 * i + 1 : 2 * i + 2]
            if two_r:
                s.activation(out=sInvA[:, i : i + 1], in_=vv, func=AF.Sqrt,
                             scale=ccst[:, 2:3], bias=ccst[:, 1:2]
                             ).then_inc(s_iv.h, 1)
            else:
                s.activation(out=sInvA[:, i : i + 1], in_=vv, func=AF.Sqrt,
                             bias=ccst[:, 0:1]).then_inc(s_iv.h, 1)
            s_iv.inc()

        def sc_finish(s, j):
            s.wait_ge(s_bn.h, bases["bn"] + j + 3)
            w2 = stw[2].ap()[:, j % 4 : j % 4 + 1]
            mu = muv.ap()[:, 2 * j : 2 * j + 1]
            s.activation(out=w2, in_=mu, func=AF.Identity, scale=ccst[:, 3:4])
            s.activation(out=sB[:, j : j + 1], in_=w2, func=AF.Identity,
                         scale=sA.ap()[:, j : j + 1])

        return bases, sc_inv, sc_finish

    def emit_ffn_stage(stage_no, src_tiles, dst_tiles, cc_idx, gate, compute_alpha,
                       psum_pred=None, post_vec=None, post_slot_gate=None,
                       post_sc_group=None):
        tag = "fS"
        bstA = [s_st4[k].n for k in range(NXW)]
        s_t = S(f"st_{tag}")
        s_tp = S(f"stp_{tag}")
        s_tev = S(f"stev_{tag}")
        s_xn = S(f"sxn_{tag}")
        s_red = S(f"sred_{tag}")
        s_mx = S(f"smx_{tag}")
        b = dict(t=s_t.n, tp=s_tp.n, tev=s_tev.n, xn=s_xn.n,
                 red=s_red.n, mx=s_mx.n)

        def slot_gate(sy, i):
            if KF1:
                sy.wait_ge(s_tp.h, b["tp"] + i - NXW + 1)
            sy.wait_ge(s_st4[i % NXW].h, bstA[i % NXW] + 16 * (i // NXW))

        def red_group(v, g):
            v.wait_ge(s_xn.h, b["xn"] + g + 1)
            v.reduce_max(
                out=maxcol[:, g : g + 1], in_=tTg[:],
                axis=AX, apply_absolute_value=True,
            )
            v.drain().then_inc(s_red.h, 1)
            s_red.inc()

        ngdone = [0]

        def vec_tile_f(v, i):
            if not KF1:
                return
            if i >= 8 and (i - 8) % 4 == 0:
                red_group(v, (i - 8) // 4)
                ngdone[0] += 1

        def vec_tail(v):
            if KF1:
                for g in range(ngdone[0], NG):
                    red_group(v, g)
                v.reduce_max(out=smax[:], in_=maxcol[:], axis=AX,
                             apply_absolute_value=True)
                v.drain().then_inc(s_mx.h, 1)
                s_mx.inc()

        bnb, sc_inv, sc_finish = emit_bn(
            tag + "a", src_tiles, st_muv, st_A, st_B, st_invA, True, gate,
            slot_gate=slot_gate, vec_tile=vec_tile_f, vec_tail=vec_tail,
        )

        def fA_sc(s):
            s.wait_ge(s_const.h, CONSTS)
            s.wait_ge(s_zr.h, 1)

            def xn_block(g):
                for c in range(NC8):
                    act = s.activation(
                        out=tTg[:, 512 * c : 512 * (c + 1)],
                        in_=tTg[:, 512 * c : 512 * (c + 1)],
                        func=AF.Identity,
                        scale=gcols[:, c : c + 1], bias=bcols[:, c : c + 1],
                    )
                    if c == NC8 - 1:
                        act.then_inc(s_xn.h, 1)
                s_xn.inc()

            def evac(j):
                if j % 4 == 0 and j >= 4:
                    s.wait_ge(s_red.h, b["red"] + j // 4)
                s.wait_ge(s_tp.h, b["tp"] + j + 1)
                s.activation(
                    out=tTg_cw[:, :, 128 * (j % 4) : 128 * (j % 4 + 1)],
                    in_=bankpair((j % 4) * 2), func=AF.Identity,
                ).then_inc(s_tev.h, 1)
                s_tev.inc()
                if j % 4 == 3:
                    xn_block(j // 4)

            nev = [0]

            def advance(upto):
                while KF1 and nev[0] <= upto:
                    evac(nev[0])
                    nev[0] += 1

            for i in range(NT + 1):
                if i < NT:
                    sc_inv(s, i)
                if i >= 1:
                    j = i - 1
                    sc_finish(s, j)
                    s.activation(
                        out=xw[j % NXW][:], in_=xw[j % NXW][:], func=AF.Identity,
                        scale=st_A[:, j : j + 1], bias=st_B[:, j : j + 1],
                    ).then_inc(s_t.h, 1)
                    s_t.inc()
                    s.wait_ge(s_t.h, s_t.n)
                    s.dma_start(out=tt[j], in_=xw[j % NXW][:]).then_inc(
                        s_st4[j % NXW].h, 16
                    )
                    s_st4[j % NXW].inc(16)
                    advance(j - 2)
            advance(NT - 1)

        def fA_te(t):
            t.wait_ge(s_const.h, CONSTS)
            if psum_pred:
                for sem, val in psum_pred:
                    t.wait_ge(sem.h, val)
            for i in range(NT):
                t.wait_ge(s_t.h, b["t"] + i + 1)
                if i >= 4:
                    t.wait_ge(s_tev.h, b["tev"] + i - 3)
                pb = (i % 4) * 2
                for c in range(NC8):
                    mm = t.matmul(
                        bank(pb + c // 4, 128 * (c % 4), 128 * (c % 4) + 128),
                        xw[i % NXW][:, 128 * c : 128 * (c + 1)],
                        identF[:],
                        start=(c % 4 == 0), stop=(c % 4 == 3), is_transpose=True,
                    )
                    if c == NC8 - 1:
                        mm.then_inc(s_tp.h, 1)
                s_tp.inc()

        sec(blk.scalar, fA_sc)
        if KF1:
            sec(blk.tensor, fA_te)

        s_ccw = S(f"sccw_{tag}")
        s_ccd = S(f"sccd_{tag}")
        s_ccr = S(f"sccr_{tag}")
        s_rw = S(f"srw_{tag}")
        s_bc = S(f"sbc_{tag}")

        def fC_gp_a(g):
            g.wait_ge(s_mx.h, b["mx"] + 1)
            g.dma_start(out=cc_in[cc_idx][0:1, :], in_=smax[:, 0:1]).then_inc(
                s_ccw.h, 16
            )
            s_ccw.inc(16)
            g.wait_ge(s_ccw.h, s_ccw.n)
            g.collective_compute(
                "AllReduce", OP.max,
                replica_groups=[list(range(n_cores))],
                ins=[cc_in[cc_idx].opt()],
                outs=[cc_out[cc_idx].opt()],
            ).then_inc(s_ccd.h, 1)
            s_ccd.inc()
            g.wait_ge(s_ccd.h, s_ccd.n)
            g.dma_start(out=grow[0:1, :], in_=cc_out[cc_idx][0:1, :]).then_inc(
                s_ccr.h, 16
            )
            s_ccr.inc(16)

        if KF1:
            sec(blk.gpsimd, fC_gp_a)
        ccr_val = s_ccr.n

        s_qs = S(f"sqs_{tag}")
        qs_base = s_qs.n

        def fC_vec1(v):
            v.wait_ge(s_ccr.h, ccr_val)
            if compute_alpha:
                v.wait_ge(s_alf.h, 16)
                ev = strided1(arow.ap()[0:1, 0:1], 2, 128)
                od = strided1(arow.ap()[0:1, 1:2], 2, 128)
                v.reduce_sum(out=s11[0:1, 4:5], in_=ev, axis=AX)
                v.drain()
                v.reduce_sum(out=s11[0:1, 5:6], in_=od, axis=AX)
                v.drain()
                v.tensor_scalar(out=s11[0:1, 6:7], in0=s11[0:1, 5:6], scalar1=1.0,
                                scalar2=None, op0=OP.max)
                v.drain()
                v.reciprocal(out=s11[0:1, 6:7], in_=s11[0:1, 6:7])
                v.drain()
                v.tensor_tensor(out=s11[0:1, 6:7], in0=s11[0:1, 6:7],
                                in1=s11[0:1, 4:5], op=OP.mult)
                v.drain()
            v.reduce_max(out=s11[0:1, 0:1], in_=grow[0:1, :], axis=AX)
            v.drain()
            v.tensor_scalar(
                out=s11[0:1, 1:2], in0=s11[0:1, 0:1],
                scalar1=float(np.float32(1e-8)), scalar2=1.0 / 127.0,
                op0=OP.max, op1=OP.mult,
            )
            v.drain()
            v.reciprocal(out=s11[0:1, 2:3], in_=s11[0:1, 1:2])
            v.drain()
            v.tensor_tensor(out=s11[0:1, 3:4], in0=s11[0:1, 1:2],
                            in1=s11[0:1, 6:7], op=OP.mult)
            v.drain()
            v.tensor_scalar(out=s11[0:1, 3:4], in0=s11[0:1, 3:4], scalar1=0.5,
                            scalar2=None, op0=OP.mult)
            v.drain()
            v.tensor_copy(brow[0:1, 0:128], rep0(s11.ap()[0:1, 2:3], 128))
            v.drain()
            v.tensor_copy(brow[0:1, 128:256], rep0(s11.ap()[0:1, 3:4], 128))
            v.drain().then_inc(s_rw.h, 1)
            s_rw.inc()

        if KF1:
            sec(blk.vector, fC_vec1)

        def fC_gp_b(g):
            g.wait_ge(s_rw.h, s_rw.n)
            g.dma_start(out=invscol[:, 0:1], in_=brow[0:1, 0:128]).then_inc(
                s_bc.h, 16
            )
            s_bc.inc(16)
            g.dma_start(out=hafcol[:, 0:1], in_=brow[0:1, 128:256]).then_inc(
                s_bc.h, 16
            )
            s_bc.inc(16)

        if KF1:
            sec(blk.gpsimd, fC_gp_b)
        bc_val = s_bc.n

        def fC_vec2(v):
            v.wait_ge(s_bc.h, bc_val)
            v.tensor_scalar(out=st_cA[:], in0=st_A[:], scalar1=hafcol[:, 0:1],
                            scalar2=None, op0=OP.mult)
            v.drain()
            v.tensor_scalar(out=gscols[:], in0=gcols[:], scalar1=invscol[:, 0:1],
                            scalar2=None, op0=OP.mult)
            v.drain()
            v.tensor_scalar(
                out=bsccols[:], in0=bcols[:], scalar1=invscol[:, 0:1],
                scalar2=None, op0=OP.mult,
            )
            v.drain().then_inc(s_qs.h, 1)
            s_qs.inc()

        if KF1:
            sec(blk.vector, fC_vec2)

        s_txf = S(f"stxf_{tag}")
        s_tp2 = S(f"stp2_{tag}")
        s_tev2 = S(f"stev2_{tag}")
        s_q8 = S(f"sq8_{tag}")
        s_nb = S(f"snb_{tag}")
        s_fm = S(f"sfm_{tag}")
        s_e1 = S(f"se1_{tag}")
        s_o = S(f"so_{tag}")
        b2 = dict(t=s_t.n, txf=s_txf.n, tp2=s_tp2.n, tev2=s_tev2.n, q8=s_q8.n,
                  nb=s_nb.n, fm=s_fm.n, e1=s_e1.n, o=s_o.n, qs=qs_base,
                  ld=[s_ld[k].n for k in range(NXW)])
        bstB = [s_st4[k].n for k in range(NXW)]

        def fB_sync(sy):
            sy.wait_ge(s_tp.h, s_tp.n)
            for k in range(NXW):
                sy.wait_ge(s_st4[k].h, bstA[k] + 16 * (NT // NXW))
            for i in range(NT):
                if i >= NXW:
                    sy.wait_ge(s_st4[i % NXW].h, bstB[i % NXW] + 16 * (i // NXW))
                    if post_slot_gate is not None:
                        post_slot_gate(sy, i)
                sy.dma_start(out=xw[i % NXW][:], in_=tt[i]).then_inc(
                    s_ld[i % NXW].h, 16
                )
                s_ld[i % NXW].inc(16)

        def fB_te(t):
            def transposes(g):
                for j in range(4):
                    i = 4 * g + j
                    t.wait_ge(s_ld[i % NXW].h, b2["ld"][i % NXW] + 16 * (i // NXW + 1))
                    if g == 0 and j == 0:
                        t.wait_ge(s_tev.h, b["tev"] + NT)
                    if g >= 1 and j == 0:
                        t.wait_ge(s_nb.h, b2["nb"] + g)
                    if i >= 2:
                        t.wait_ge(s_tev2.h, b2["tev2"] + i - 1)
                    pb = (i % 2) * 2
                    for c in range(NC8):
                        mm = t.matmul(
                            bank(pb + c // 4, 128 * (c % 4), 128 * (c % 4) + 128),
                            xw[i % NXW][:, 128 * c : 128 * (c + 1)],
                            identF[:],
                            start=(c % 4 == 0), stop=(c % 4 == 3), is_transpose=True,
                        )
                        if c == NC8 - 1:
                            mm.then_inc(s_tp2.h, 1)
                    s_tp2.inc()

            def ffns(g):
                for j in range(4):
                    i = 4 * g + j
                    t.wait_ge(s_nb.h, b2["nb"] + g + 1)
                    if i == 0:
                        t.wait_ge(s_wsg.h, NC8)
                    if i >= 2:
                        t.wait_ge(s_e1.h, b2["e1"] + i - 1)
                    for oh in range(2):
                        for c in range(NC8):
                            mm = t.matmul(
                                bank(4 + (i % 2) * 2 + oh),
                                nTg[:, 512 * c + 128 * j : 512 * c + 128 * (j + 1)],
                                wsgn[:, c, 512 * oh : 512 * (oh + 1)],
                                start=(c == 0), stop=(c == NC8 - 1),
                            )
                            if oh == 1 and c == NC8 - 1:
                                mm.then_inc(s_fm.h, 1)
                    s_fm.inc()

            transposes(0)
            for g in range(1, NG):
                ffns(g - 1)
                transposes(g)
            ffns(NG - 1)

        def fB_vec(v):
            for g in range(NG):
                v.wait_ge(s_q8.h, b2["q8"] + g + 1)
                v.tensor_scalar(
                    out=nTg[:], in0=tTg[:], scalar1=C_ROUND, scalar2=C_ROUND,
                    op0=OP.add, op1=OP.subtract,
                )
                v.drain().then_inc(s_nb.h, 1)
                s_nb.inc()
                for j in range(4):
                    i = 4 * g + j
                    v.wait_ge(s_fm.h, b2["fm"] + i + 1)
                    v.scalar_tensor_tensor(
                        out=xw[i % NXW][:], in0=bankpair(4 + (i % 2) * 2),
                        scalar=st_cA[:, i : i + 1], in1=xw[i % NXW][:],
                        op0=OP.mult, op1=OP.add,
                    )
                    v.drain().then_inc(s_e1.h, 1)
                    s_e1.inc()
                    if post_vec is not None:
                        post_vec(v, i, b2["t"] + i + 1)
            if post_vec is not None:
                post_vec(v, NT, 0)

        def fB_sc(s):
            s.wait_ge(s_qs.h, b2["qs"] + 1)
            for g in range(NG):
                for j in range(4):
                    i = 4 * g + j
                    s.wait_ge(s_tp2.h, b2["tp2"] + i + 1)
                    s.activation(
                        out=tTg_cw[:, :, 128 * j : 128 * (j + 1)],
                        in_=bankpair((i % 2) * 2), func=AF.Identity,
                    ).then_inc(s_tev2.h, 1)
                    s_tev2.inc()
                for c in range(NC8):
                    act = s.activation(
                        out=tTg[:, 512 * c : 512 * (c + 1)],
                        in_=tTg[:, 512 * c : 512 * (c + 1)],
                        func=AF.Identity,
                        scale=gscols[:, c : c + 1], bias=bsccols[:, c : c + 1],
                    )
                    if c == NC8 - 1:
                        act.then_inc(s_q8.h, 1)
                s_q8.inc()
                for j in range(4):
                    i = 4 * g + j
                    s.wait_ge(s_e1.h, b2["e1"] + i + 1)
                    s.activation(
                        out=xw[i % NXW][:], in_=xw[i % NXW][:], func=AF.Identity,
                        scale=st_invA[:, i : i + 1],
                        bias=st_muv[:, 2 * i : 2 * i + 1],
                    ).then_inc(s_t.h, 1)
                    s_t.inc()
                    s.wait_ge(s_t.h, s_t.n)
                    s.dma_start(out=dst_tiles[i], in_=xw[i % NXW][:]).then_inc(
                        s_st4[i % NXW].h, 16
                    )
                    s_st4[i % NXW].inc(16)
                    if post_sc_group is not None:
                        post_sc_group(s, i)
            if post_sc_group is not None:
                post_sc_group(s, NT)

        if KF2:
            sec(blk.sync, fB_sync)
            sec(blk.tensor, fB_te)
            sec(blk.vector, fB_vec)
            sec(blk.scalar, fB_sc)
        return bstB

    s_bn2 = S("s_bn2")
    s_iv2 = S("s_iv2")
    s_sA2 = S("s_sA2")

    def post_vec1(v, i, tval):
        if i < NT:
            v.wait_ge(S("st_fS").h, tval)
            v.bn_stats(out=bn12[:, 0:6], in_=xw[i % NXW][:, 0:512])
            v.bn_stats(out=bn12[:, 6:12], in_=xw[i % NXW][:, 512:1024])
            v.drain()
            v.bn_aggr(out=s2_muv[:, 2 * i : 2 * i + 2], in_=bn12[:])
            v.drain().then_inc(s_bn2.h, 1)
            s_bn2.inc()
        if i >= 1:
            j = i - 1
            v.wait_ge(s_iv2.h, j + 1)
            v.reciprocal(out=s2_A[:, j : j + 1],
                         in_=stw[3][:, j % 4 : j % 4 + 1])
            v.drain().then_inc(s_sA2.h, 1)
            s_sA2.inc()

    def post_sc1(s, i):
        if i < NT:
            s.wait_ge(s_bn2.h, i + 1)
            vv = s2_muv.ap()[:, 2 * i + 1 : 2 * i + 2]
            s.activation(out=stw[3].ap()[:, i % 4 : i % 4 + 1], in_=vv,
                         func=AF.Sqrt, bias=ccst[:, 0:1]).then_inc(s_iv2.h, 1)
            s_iv2.inc()
        if i >= 1:
            j = i - 1
            s.wait_ge(s_sA2.h, j + 1)
            w2 = stw[2].ap()[:, j % 4 : j % 4 + 1]
            mu = s2_muv.ap()[:, 2 * j : 2 * j + 1]
            s.activation(out=w2, in_=mu, func=AF.Identity, scale=ccst[:, 3:4])
            s.activation(out=s2_B[:, j : j + 1], in_=w2, func=AF.Identity,
                         scale=s2_A.ap()[:, j : j + 1])

    def post_slot1(sy, i):
        sy.wait_ge(s_bn2.h, i - NXW + 1)

    bstX1 = emit_ffn_stage(1, [xt[i] for i in range(NT)],
                           [x1t[i] for i in range(NT)], 0, None, True,
                           post_vec=post_vec1, post_slot_gate=post_slot1,
                           post_sc_group=post_sc1)

    s_wqk = S("s_wqk")

    def wqkv_gp(g):
        g.wait_ge(s_as2.h, NC8)
        for c in range(NC8):
            g.dma_start(
                out=wqkv[:, c, :], in_=wqkvT_h[128 * c : 128 * (c + 1), :]
            ).then_inc(s_wqk.h, 16)
            s_wqk.inc(16)

    sec(blk.gpsimd, wqkv_gp)

    if stages < 2:
        def fin_sync1(sy):
            for k in range(NXW):
                sy.wait_ge(s_st4[k].h, s_st4[k].n)
        sec(blk.sync, fin_sync1)
        stack.close()
        return nc

    s_h2 = S("sh2_a")
    s_h2p = S("sh2p_a")
    s_h2e = S("sh2e_a")
    s_kp = S("skp_a")
    s_kmin = S("skmin_a")
    s_kexp = S("skexp_a")
    s_kev = S("skev_a")
    s_ka = S("ska_a")
    s_kadd = S("skadd_a")
    s_kvb = S("skvb_a")
    aB = dict(h2=s_h2.n, h2p=s_h2p.n, h2e=s_h2e.n, kp=s_kp.n, kmin=s_kmin.n,
              kexp=s_kexp.n, kev=s_kev.n, ka=s_ka.n, kadd=s_kadd.n,
              ld=[s_ld[k].n for k in range(NXW)])
    PRED_2A = [(S("stev2_fS"), S("stev2_fS").n), (S("se1_fS"), S("se1_fS").n)]

    def s2a_sync(sy):
        for i in range(NT):
            if i < NXW:
                sy.wait_ge(s_bn2.h, NT)
                sy.wait_ge(s_st4[i % NXW].h, bstX1[i % NXW] + 16 * (NT // NXW))
            else:
                sy.wait_ge(s_h2.h, aB["h2"] + i - NXW + 1)
                sy.wait_ge(s_st4[i % NXW].h, bstX1[i % NXW] + 16 * (i // NXW + 1))
            sy.dma_start(out=xw[i % NXW][:], in_=x1t[i]).then_inc(
                s_ld[i % NXW].h, 16
            )
            s_ld[i % NXW].inc(16)

    sec(blk.sync, s2a_sync)

    def s2a_sc(s):
        s.wait_ge(s_sA2.h, NT)

        def h2_tile(i):
            s.wait_ge(s_ld[i % NXW].h, aB["ld"][i % NXW] + 16 * (i // NXW + 1))
            if i >= 2:
                s.wait_ge(s_h2p.h, aB["h2p"] + i - 1)
            s.activation(
                out=h2t[i % 2][:], in_=xw[i % NXW][:], func=AF.Identity,
                scale=s2_A[:, i : i + 1], bias=s2_B[:, i : i + 1],
            ).then_inc(s_h2.h, 1)
            s_h2.inc()

        def exp_tile(i):
            for og in range(2):
                s.wait_ge(s_kmin.h, aB["kmin"] + 2 * i + og + 1)
                s.activation(
                    out=scrB[:, 512 * og : 512 * (og + 1)],
                    in_=scrA[:, 512 * og : 512 * (og + 1)],
                    func=AF.Exp,
                ).then_inc(s_kexp.h, 1)
                s_kexp.inc()

        for i in range(min(4, NT)):
            h2_tile(i)
        for i in range(4, NT):
            h2_tile(i)
            exp_tile(i - 4)
        for i in range(max(0, NT - 4), NT):
            exp_tile(i)

    sec(blk.scalar, s2a_sc)

    def s2a_te(t):
        def transp_tile(i):
            t.wait_ge(s_h2.h, aB["h2"] + i + 1)
            if i == 0:
                t.wait_ge(s_wqk.h, 16 * NC8)
                for sem, val in PRED_2A:
                    t.wait_ge(sem.h, val)
            if i >= 4:
                t.wait_ge(s_h2e.h, aB["h2e"] + i - 3)
            for c in range(NC8):
                mm = t.matmul(
                    bankb16(i % 4)[:, 128 * c : 128 * (c + 1)],
                    h2t[i % 2][:, 128 * c : 128 * (c + 1)],
                    identB[:],
                    start=(c == 0), stop=(c == NC8 - 1), is_transpose=True,
                )
                if c == NC8 - 1:
                    mm.then_inc(s_h2p.h, 1)
            s_h2p.inc()

        def kv_tile(i):
            t.wait_ge(s_h2e.h, aB["h2e"] + i + 1)
            for og in range(4):
                idx = 4 * i + og
                if idx >= 2:
                    t.wait_ge(s_kev.h, aB["kev"] + idx - 1)
                for c in range(NC8):
                    mm = t.matmul(
                        bank(4 + og % 2),
                        h2Tt[i % 4][:, 128 * c : 128 * (c + 1)],
                        wqkv[:, c, D + 512 * og : D + 512 * (og + 1)],
                        start=(c == 0), stop=(c == NC8 - 1),
                    )
                    if c == NC8 - 1:
                        mm.then_inc(s_kp.h, 1)
                s_kp.inc()
            t.wait_ge(s_kev.h, aB["kev"] + 4 * (i + 1))
            for q in range(2):
                if 2 * i + q >= 1:
                    t.wait_ge(s_kadd.h, aB["kadd"] + 2 * i + q)
                for c in range(4 * q, 4 * q + 4):
                    cc = c - 4 * q
                    mm = t.matmul(
                        bank(6 + cc // 2, 130 * (cc % 2), 130 * (cc % 2) + 130),
                        ktile[i % 2][:, 128 * c : 128 * (c + 1)],
                        vsb[i % 2][:, 130 * c : 130 * (c + 1)],
                        start=(cc % 2 == 0), stop=(cc % 2 == 1),
                    )
                    if cc == 3:
                        mm.then_inc(s_ka.h, 1)
                s_ka.inc()

        for i in range(min(4, NT)):
            transp_tile(i)
        for i in range(NT - 4):
            kv_tile(i)
            transp_tile(i + 4)
        for i in range(max(0, NT - 4), NT):
            kv_tile(i)

    sec(blk.tensor, s2a_te)

    def s2a_vec(v):
        def evac_tile2(i):
            v.wait_ge(s_h2p.h, aB["h2p"] + i + 1)
            if i >= 4:
                v.wait_ge(s_kp.h, aB["kp"] + 4 * (i - 3))
            v.tensor_copy(h2Tt[i % 4][:], bankb16(i % 4))
            v.drain().then_inc(s_h2e.h, 1)
            s_h2e.inc()

        def kvelem_tile2(i):
            for og in range(4):
                idx = 4 * i + og
                v.wait_ge(s_kp.h, aB["kp"] + idx + 1)
                if i >= 2 and og == 0:
                    v.wait_ge(s_ka.h, aB["ka"] + 2 * (i - 1))
                pk = bank(4 + og % 2)
                if og < 2:
                    v.tensor_scalar(
                        out=scrA[:, 512 * og : 512 * (og + 1)], in0=pk,
                        scalar1=0.0, scalar2=None, op0=OP.min,
                    )
                    v.drain().then_inc(s_kmin.h, 1)
                    s_kmin.inc()
                    v.wait_ge(s_kexp.h, aB["kexp"] + 2 * i + og + 1)
                    v.scalar_tensor_tensor(
                        out=ktile[i % 2][:, 512 * og : 512 * (og + 1)], in0=pk,
                        scalar=0.0, in1=scrB[:, 512 * og : 512 * (og + 1)],
                        op0=OP.max, op1=OP.add,
                    )
                    v.drain().then_inc(s_kev.h, 1)
                    s_kev.inc()
                else:
                    vo = og - 2
                    dst = vsb[i % 2].ap().rearrange("p (h c) -> p h c", c=65)[
                        :, 8 * vo : 8 * (vo + 1), 0:64
                    ]
                    v.tensor_copy(dst, pk)
                    v.drain().then_inc(s_kev.h, 1)
                    s_kev.inc()
            for q in range(2):
                v.wait_ge(s_ka.h, aB["ka"] + 2 * i + q + 1)
                psrc = PS.ap()[:, 512 * 6 : 512 * 8].rearrange(
                    "p (b s) -> p b s", b=2)[:, :, 0:260]
                kdst = kvf.ap()[:, 520 * q : 520 * (q + 1)].rearrange(
                    "p (b c) -> p b c", b=2)
                v.tensor_tensor(out=kdst, in0=kdst, in1=psrc, op=OP.add)
                v.drain().then_inc(s_kadd.h, 1)
                s_kadd.inc()

        for i in range(min(4, NT)):
            evac_tile2(i)
        for i in range(4, NT):
            kvelem_tile2(i - 4)
            evac_tile2(i)
        for i in range(max(0, NT - 4), NT):
            kvelem_tile2(i)
        kv3 = kvf.ap().rearrange("p (c w) -> p c w", w=130)
        v.memset(kv3[64:128, :, 0:65], 0.0)
        v.drain()
        v.memset(kv3[0:64, :, 65:130], 0.0)
        v.drain()
        v.tensor_copy(kvb[:], kvf[:])
        v.drain().then_inc(s_kvb.h, 1)
        s_kvb.inc()

    sec(blk.vector, s2a_vec)

    s_wpk = S("s_wpk")

    def wproj_gp(g):
        for c in range(NC8):
            g.dma_start(
                out=wproj[:, c, :], in_=wprojT_h[128 * c : 128 * (c + 1), :]
            ).then_inc(s_wpk.h, 16)
            s_wpk.inc(16)

    sec(blk.gpsimd, wproj_gp)

    s_xf2b = S("sxf_s2b")
    s_h2b = S("sh2_b")
    s_h2pb = S("sh2p_b")
    s_h2eb = S("sh2e_b")
    s_qm = S("sqm_b")
    s_qmin = S("sqmin_b")
    s_qexp = S("sqexp_b")
    s_qe = S("sqe_b")
    s_nd = S("snd_b")
    s_dv = S("sdv_b")
    s_ap = S("sap_b")
    s_ae = S("sae_b")
    s_pj = S("spj_b")
    s_rs = S("srs_b")
    s_o2 = S("so_s2b")
    bB = dict(xf=s_xf2b.n, h2=s_h2b.n, h2p=s_h2pb.n, h2e=s_h2eb.n,
              qm=s_qm.n, qmin=s_qmin.n, qexp=s_qexp.n, qe=s_qe.n, nd=s_nd.n,
              dv=s_dv.n, ap=s_ap.n, ae=s_ae.n, pj=s_pj.n, rs=s_rs.n, o=s_o2.n,
              ld=[s_ld[k].n for k in range(NXW)])
    bstX2 = [s_st4[k].n for k in range(NXW)]

    def s2b_sync(sy):
        sy.wait_ge(s_h2.h, s_h2.n)
        for i in range(NT):
            if i >= NXW:
                sy.wait_ge(s_st4[i % NXW].h, bstX2[i % NXW] + 16 * (i // NXW))
            sy.dma_start(out=xw[i % NXW][:], in_=x1t[i]).then_inc(
                s_ld[i % NXW].h, 16
            )
            s_ld[i % NXW].inc(16)

    def s2b_sc(s):
        for g in range(NG):
            for j in range(4):
                i = 4 * g + j
                s.wait_ge(s_ld[i % NXW].h, bB["ld"][i % NXW] + 16 * (i // NXW + 1))
                if i >= 2:
                    s.wait_ge(s_h2pb.h, bB["h2p"] + i - 1)
                s.activation(
                    out=h2t[i % 2][:], in_=xw[i % NXW][:], func=AF.Identity,
                    scale=s2_A[:, i : i + 1], bias=s2_B[:, i : i + 1],
                ).then_inc(s_h2b.h, 1)
                s_h2b.inc()
            for oc in range(NC8):
                idx = NC8 * g + oc
                s.wait_ge(s_qmin.h, bB["qmin"] + idx + 1)
                s.activation(out=scrB[:, 0:512], in_=scrA[:, 0:512],
                             func=AF.Exp).then_inc(s_qexp.h, 1)
                s_qexp.inc()
            for j in range(4):
                i = 4 * g + j
                s.wait_ge(s_rs.h, bB["rs"] + i + 1)
                s.dma_start(out=x2t[i], in_=xw[i % NXW][:]).then_inc(
                    s_st4[i % NXW].h, 16
                )
                s_st4[i % NXW].inc(16)

    PRED_2B = [(s_h2e, s_h2e.n), (s_kev, s_kev.n), (s_kadd, s_kadd.n)]

    def s2b_te(t):
        for g in range(NG):
            for j in range(4):
                i = 4 * g + j
                t.wait_ge(s_h2b.h, bB["h2"] + i + 1)
                if g == 0 and j == 0:
                    for sem, val in PRED_2B:
                        t.wait_ge(sem.h, val)
                if g >= 1 and j == 0:
                    t.wait_ge(s_pj.h, bB["pj"] + 8 * g)
                if i >= 2:
                    t.wait_ge(s_h2eb.h, bB["h2e"] + i - 1)
                for c in range(NC8):
                    mm = t.matmul(
                        bankb16(i % 2)[:, 128 * c : 128 * (c + 1)],
                        h2t[i % 2][:, 128 * c : 128 * (c + 1)],
                        identB[:],
                        start=(c == 0), stop=(c == NC8 - 1), is_transpose=True,
                    )
                    if c == NC8 - 1:
                        mm.then_inc(s_h2pb.h, 1)
                s_h2pb.inc()
            for oc in range(NC8):
                idx = NC8 * g + oc
                t.wait_ge(s_h2eb.h, bB["h2e"] + 4 * (g + 1))
                if idx == 0:
                    t.wait_ge(s_kvb.h, 1)
                if idx >= 2:
                    t.wait_ge(s_qe.h, bB["qe"] + idx - 1)
                for c in range(NC8):
                    mm = t.matmul(
                        bank(2 + oc % 2),
                        wqkv[:, c, 128 * oc : 128 * (oc + 1)],
                        h2Tg[:, 512 * c : 512 * (c + 1)],
                        start=(c == 0), stop=(c == NC8 - 1),
                    )
                    if c == NC8 - 1:
                        mm.then_inc(s_qm.h, 1)
                s_qm.inc()
                t.wait_ge(s_qe.h, bB["qe"] + idx + 1)
                if g >= 1 and oc == 0:
                    t.wait_ge(s_ae.h, bB["ae"] + 4 * g)
                    t.wait_ge(s_rs.h, bB["rs"] + 4 * g)
                for ts in range(4):
                    nd_idx = idx * 4 + ts
                    if nd_idx >= 4:
                        t.wait_ge(s_dv.h, bB["dv"] + nd_idx - 3)
                    t.matmul(
                        bank(4 + ts, 0, 130),
                        qg[:, 512 * oc + 128 * ts : 512 * oc + 128 * (ts + 1)],
                        kvb[:, 130 * oc : 130 * (oc + 1)],
                        start=True, stop=True,
                    ).then_inc(s_nd.h, 1)
                    s_nd.inc()
            for j in range(4):
                i = 4 * g + j
                t.wait_ge(s_dv.h, bB["dv"] + 32 * (g + 1))
                if i >= 1:
                    t.wait_ge(s_ae.h, bB["ae"] + i)
                for c in range(NC8):
                    mm = t.matmul(
                        bankb16(4)[:, 128 * c : 128 * (c + 1)],
                        attng_ap[:, 1024 * j + 128 * c : 1024 * j + 128 * (c + 1)],
                        identB[:],
                        start=(c == 0), stop=(c == NC8 - 1), is_transpose=True,
                    )
                    if c == NC8 - 1:
                        mm.then_inc(s_ap.h, 1)
                s_ap.inc()
                t.wait_ge(s_ae.h, bB["ae"] + i + 1)
                if i == 0:
                    t.wait_ge(s_wpk.h, 16 * NC8)
                    t.wait_ge(s_bpj.h, 16)
                for oh in range(2):
                    pidx = 2 * i + oh
                    if pidx >= 2:
                        t.wait_ge(s_rs.h, bB["rs"] + pidx // 2)
                    for c in range(NC8):
                        t.matmul(
                            bank(5 + oh),
                            h2Tg[:, 512 * c + 128 * j : 512 * c + 128 * (j + 1)],
                            wproj[:, c, 512 * oh : 512 * (oh + 1)],
                            start=(c == 0), stop=False,
                        )
                    t.matmul(
                        bank(5 + oh),
                        ones1[:, :],
                        bprojb[0:1, 512 * oh : 512 * (oh + 1)],
                        start=False, stop=True,
                    ).then_inc(s_pj.h, 1)
                    s_pj.inc()

    def s2b_vec(v):
        _a = PS.ap()[:, 512 * 4 + 64 : 512 * 4 + 65]
        nd_dencols = bass.AP(tensor=_a.tensor, offset=_a.offset,
                             ap=[list(_a.ap[0]), [512, 4], [65, 2]])
        rcb8 = rcb.ap()[:, 0:8].rearrange("p (b h) -> p b h", b=4)
        for g in range(NG):
            for j in range(4):
                i = 4 * g + j
                v.wait_ge(s_h2pb.h, bB["h2p"] + i + 1)
                v.tensor_copy(
                    h2Tg_cw[:, :, 128 * j : 128 * (j + 1)], bankb16(i % 2)
                )
                v.drain().then_inc(s_h2eb.h, 1)
                s_h2eb.inc()
            for oc in range(NC8):
                idx = NC8 * g + oc
                v.wait_ge(s_qm.h, bB["qm"] + idx + 1)
                pq = bank(2 + oc % 2)
                v.tensor_scalar(out=scrA[:, 0:512], in0=pq, scalar1=0.0,
                                scalar2=None, op0=OP.min)
                v.drain().then_inc(s_qmin.h, 1)
                s_qmin.inc()
                v.wait_ge(s_qexp.h, bB["qexp"] + idx + 1)
                v.scalar_tensor_tensor(
                    out=qg[:, 512 * oc : 512 * (oc + 1)], in0=pq, scalar=0.0,
                    in1=scrB[:, 0:512], op0=OP.max, op1=OP.add,
                )
                v.drain().then_inc(s_qe.h, 1)
                s_qe.inc()
                v.wait_ge(s_nd.h, bB["nd"] + 4 * idx + 4)
                v.tensor_scalar(out=rcb8, in0=nd_dencols,
                                scalar1=float(np.float32(1e-8)),
                                scalar2=None, op0=OP.add)
                v.drain()
                v.reciprocal(out=rrb[:, 0:8], in_=rcb[:, 0:8])
                v.drain()
                for ts in range(4):
                    ndap = PS.ap()[:, 512 * (4 + ts) : 512 * (4 + ts) + 130]
                    for hh in range(2):
                        v.tensor_scalar(
                            out=attng_ap[:, 1024 * ts + 128 * oc + 64 * hh :
                                         1024 * ts + 128 * oc + 64 * (hh + 1)],
                            in0=ndap[:, 65 * hh : 65 * hh + 64],
                            scalar1=rrb[:, 2 * ts + hh : 2 * ts + hh + 1],
                            scalar2=None, op0=OP.mult,
                        )
                v.drain().then_inc(s_dv.h, 4)
                s_dv.inc(4)
            for j in range(4):
                i = 4 * g + j
                v.wait_ge(s_ap.h, bB["ap"] + i + 1)
                v.tensor_copy(
                    h2Tg_cw[:, :, 128 * j : 128 * (j + 1)], bankb16(4)
                )
                v.drain().then_inc(s_ae.h, 1)
                s_ae.inc()
                v.wait_ge(s_pj.h, bB["pj"] + 2 * i + 2)
                v.scalar_tensor_tensor(
                    out=xw[i % NXW][:], in0=PS.ap()[:, 512 * 5 : 512 * 7],
                    scalar=0.0, in1=xw[i % NXW][:],
                    op0=OP.bypass, op1=OP.add,
                )
                v.drain().then_inc(s_rs.h, 1)
                s_rs.inc()

    sec(blk.sync, s2b_sync)
    sec(blk.scalar, s2b_sc)
    sec(blk.tensor, s2b_te)
    sec(blk.vector, s2b_vec)

    if stages < 3:
        def fin_sync2(sy):
            for k in range(NXW):
                sy.wait_ge(s_st4[k].h, s_st4[k].n)
        sec(blk.sync, fin_sync2)
        stack.close()
        return nc

    def gate_x2(sy, i):
        if i < NXW:
            sy.wait_ge(s_st4[i % NXW].h, bstX2[i % NXW] + 16 * (NT // NXW))
        else:
            sy.wait_ge(s_st4[i % NXW].h, bstX2[i % NXW] + 16 * (i // NXW + 1))

    emit_ffn_stage(3, [x2t[i] for i in range(NT)],
                   [outt[i] for i in range(NT)], 1, gate_x2, False,
                   psum_pred=[(s_rs, s_rs.n), (s_dv, s_dv.n), (s_ae, s_ae.n),
                              (s_qe, s_qe.n), (s_h2eb, s_h2eb.n)])

    def fin_sync(sy):
        for k in range(NXW):
            sy.wait_ge(s_st4[k].h, s_st4[k].n)

    sec(blk.sync, fin_sync)

    stack.close()
    return nc



def _prep_inputs(L0_attn_x, L0_attn_w_qkv, L0_attn_w_proj, L0_attn_b_proj,
                 L0_ffn_weight, L0_ffn_gamma, L0_ffn_beta):
    wqkvT = np.ascontiguousarray(np.asarray(L0_attn_w_qkv).T).astype(np.float32)
    wprojT = np.ascontiguousarray(np.asarray(L0_attn_w_proj).T).astype(np.float32)
    wffnT = np.ascontiguousarray(np.asarray(L0_ffn_weight).T).astype(np.float32)
    gcols = np.ascontiguousarray(
        np.asarray(L0_ffn_gamma).astype(np.float32).reshape(NC8, 128).T)
    bcols = np.ascontiguousarray(
        np.asarray(L0_ffn_beta).astype(np.float32).reshape(NC8, 128).T)
    bprojr = np.ascontiguousarray(
        np.asarray(L0_attn_b_proj).astype(np.float32).reshape(1, D))
    shared = dict(
        wqkvT=wqkvT, wprojT=wprojT, wffnT=wffnT, gcolsi=gcols, bcolsi=bcols,
        bprojr=bprojr,
        identF=np.eye(128, dtype=np.float32),
        identB=np.eye(128, dtype=ml_dtypes.bfloat16),
        ones1=np.ones((1, 128), dtype=ml_dtypes.bfloat16),
    )
    xs = np.asarray(L0_attn_x)
    in_maps = []
    for bidx in range(xs.shape[0]):
        m = dict(shared)
        m["x_in"] = np.ascontiguousarray(xs[bidx].astype(np.float32))
        in_maps.append(m)
    return in_maps


def kernel(L0_attn_x, L0_attn_w_qkv, L0_attn_w_proj, L0_attn_b_proj,
           L0_ffn_weight, L0_ffn_gamma, L0_ffn_beta):
    n_cores = 8
    in_maps = _prep_inputs(L0_attn_x, L0_attn_w_qkv, L0_attn_w_proj,
                           L0_attn_b_proj, L0_ffn_weight, L0_ffn_gamma,
                           L0_ffn_beta)
    nc = build_nc(T=4096, n_cores=n_cores)
    res = run_bass_kernel_spmd(nc, in_maps, core_ids=list(range(n_cores)))
    out = np.stack([np.asarray(r["out"]) for r in res.results], axis=0)
    return out.astype(np.float32)
